# version 07
# baseline (speedup 1.0000x reference)
import contextlib

import numpy as np
import ml_dtypes
import concourse.bass as bass
import concourse.mybir as mybir
from concourse.bass_utils import run_bass_kernel_spmd

F32 = mybir.dt.float32
BF16 = mybir.dt.bfloat16
U8 = mybir.dt.uint8
AF = mybir.ActivationFunctionType
ALU = mybir.AluOpType
AX = mybir.AxisListType

T = 2048
D = 2048
HEADS = 16
KVN = 4096
II = 8192
NFC = D // 128
NIC = II // 128
NTC = T // 128
SCALE_ATT = float(1.0 / np.float32(np.sqrt(np.float32(128.0))))
C_FWHT1 = float(1.0 / (64.0 * np.sqrt(2048.0)))
C_FWHT2 = float(1.0 / (64.0 * np.sqrt(8192.0)))
LN_EPS = 1e-5

ENGINES = ("sync", "vector", "scalar", "tensor", "gpsimd")
STAGE = 6
DEBUG = False


def hadamard(n):
    h = np.array([[1.0]], dtype=np.float64)
    while h.shape[0] < n:
        h = np.block([[h, h], [h, -h]])
    return h


def _dma(dst, src):
    return lambda e: e.dma_start(out=dst, in_=src)


def _dma_nc(dst, src):
    return lambda e: e.dma_start(out=dst, in_=src,
                                 allow_slow_non_contiguous=True)


def _dma_acc(dst, src):
    return lambda e: e.dma_start(out=dst, in_=src, accum_op=ALU.add)


def _mm(dst, w, r, a0, st):
    return lambda e: e.matmul(dst, w, r, start=a0, stop=st)


def _tr(dst, src, ident, a0=True, st=True):
    return lambda e: e.matmul(dst, src, ident, start=a0, stop=st,
                              is_transpose=True)


def _cp(dst, src):
    return lambda e: e.tensor_copy(dst, src)


def _scp(dst, src):
    return lambda e: e.copy(dst, src)


def _act(dst, src, f, bias=0.0, scale=1.0, acc=None):
    return lambda e: e.activation(dst, src, f, bias=bias, scale=scale,
                                  accum_out=acc)


def _ts(dst, src, s1, s2, op0, op1):
    return lambda e: e.tensor_scalar(dst, src, s1, s2, op0, op1)


def _tsm(dst, src, s):
    return lambda e: e.tensor_scalar_mul(dst, src, s)


def _stt(dst, in0, s, in1, op0, op1):
    return lambda e: e.scalar_tensor_tensor(dst, in0, s, in1, op0, op1)


def _sub(dst, a, b):
    return lambda e: e.tensor_sub(dst, a, b)


def _add(dst, a, b):
    return lambda e: e.tensor_add(dst, a, b)


def _mul(dst, a, b):
    return lambda e: e.tensor_mul(dst, a, b)


def _rmax(dst, src, neg):
    return lambda e: e.reduce_max(dst, src, axis=AX.X, negate=neg)


def _rsum(dst, src):
    return lambda e: e.reduce_sum(dst, src, axis=AX.X)


def _recip(dst, src):
    return lambda e: e.reciprocal(dst, src)


def _memset(dst, c):
    return lambda e: e.memset(dst, c)


class Prog:
    def __init__(self, semnames=()):
        self.q = {e: [] for e in ENGINES}
        self.cnt = {}
        for s in semnames:
            self.cnt[s] = 0

    def emit(self, eng, fn, wait=(), inc=None, inc_by=1):
        wl = []
        for s, v in wait:
            assert v <= self.cnt.get(s, 0), (s, v, self.cnt.get(s, 0))
            if v > 0:
                wl.append((s, int(v)))
        r = None
        if inc is not None:
            self.cnt[inc] = self.cnt.get(inc, 0) + inc_by
            r = self.cnt[inc]
        self.q[eng].append((fn, tuple(wl), inc, inc_by))
        return r

    def now(self, sem):
        return self.cnt.get(sem, 0)

    def fence(self, *sems):
        if not sems:
            sems = tuple(self.cnt.keys())
        return [(s, self.cnt[s]) for s in sems if self.cnt.get(s, 0) > 0]

    def replay(self, block, sems):
        def mk(name):
            def body(eng):
                seen = {}
                for fn, waits, inc, inc_by in self.q[name]:
                    for s, val in waits:
                        if seen.get(s, 0) < val:
                            eng.wait_ge(sems[s], val)
                            seen[s] = val
                    ins = fn(eng)
                    if inc is not None:
                        ins.then_inc(sems[inc], inc_by)
            return body
        block.sync(mk("sync"))
        block.vector(mk("vector"))
        block.scalar(mk("scalar"))
        block.tensor(mk("tensor"))
        block.gpsimd(mk("gpsimd"))


class Carver:
    def __init__(self, nc):
        self.nc = nc
        self.items = []

    def tensor(self, arena_key, name, shape, dtype, offset, space="SB"):
        if space == "PSUM":
            h = self.nc.virtual_psum_tensor(name, shape, dtype)
        else:
            h = self.nc.virtual_sbuf_tensor(name, shape, dtype)
        per_part = int(np.prod(shape[1:])) * mybir.dt.size(dtype)
        self.items.append((arena_key, h, offset, offset + per_part, space))
        return h.ap()

    def materialize(self, reals):
        for key, h, off, end, space in self.items:
            real = reals[key]
            per_real = (int(np.prod(real.shape[1:])) *
                        mybir.dt.size(real.dtype))
            assert end <= per_real, (h.name, end, per_real)
            self.nc._tensor(h.name, list(h.shape), h.dtype, type=space)
            mloc = self.nc.lookup_mloc(h)
            mloc.allocated = not self.nc.target_bir_lowering
            mloc.addr = self.nc.lookup_mloc(real).addr + off


def _ap_of(h):
    return h.ap()


def _finish(nc, ctx, P):
    sems = {n: ctx.enter_context(nc.semaphore(n)) for n in sorted(P.cnt)}
    with nc.Block() as block:
        P.replay(block, sems)


def build_nc():
    nc = bass.Bass("TRN2", target_bir_lowering=False)

    x_d = nc.dram_tensor("x", [T, D], F32, kind="ExternalInput").ap()
    kvk_d = nc.dram_tensor("kvk", [KVN, D], F32, kind="ExternalInput").ap()
    kvv_d = nc.dram_tensor("kvv", [KVN, D], F32, kind="ExternalInput").ap()
    wq_d = nc.dram_tensor("wq", [D, D], F32, kind="ExternalInput").ap()
    wo_d = nc.dram_tensor("wo", [D, D], F32, kind="ExternalInput").ap()
    suu_d = nc.dram_tensor("su_ug", [D], F32, kind="ExternalInput").ap()
    sud_d = nc.dram_tensor("su_dp", [II], F32, kind="ExternalInput").ap()
    wsu_d = nc.dram_tensor("wscale_ug", [2 * II], F32, kind="ExternalInput").ap()
    wsd_d = nc.dram_tensor("wscale_dp", [D], F32, kind="ExternalInput").ap()
    wup_d = nc.dram_tensor("wup", [II, D], F32, kind="ExternalInput").ap()
    wg_d = nc.dram_tensor("wg", [II, D], F32, kind="ExternalInput").ap()
    wd_d = nc.dram_tensor("wd", [D, II], F32, kind="ExternalInput").ap()
    out_d = nc.dram_tensor("out", [T, D], F32, kind="ExternalOutput").ap()

    SCR = "ExternalOutput" if DEBUG else "Internal"
    if DEBUG:
        xnh_dbg = nc.dram_tensor("xnh_dbg", [128, NFC * T], BF16,
                                 kind="ExternalOutput").ap()
        xnl_dbg = nc.dram_tensor("xnl_dbg", [128, NFC * T], BF16,
                                 kind="ExternalOutput").ap()
        st_dbg = nc.dram_tensor("st_dbg", [NTC, 128, 8], F32,
                                 kind="ExternalOutput").ap()
        kfm_dbg = nc.dram_tensor("kfm_dbg", [128, 2 * KVN], BF16,
                                 kind="ExternalOutput").ap()
        vtm_dbg = nc.dram_tensor("vtm_dbg", [128, 2 * KVN], BF16,
                                 kind="ExternalOutput").ap()
        p_dbg = nc.dram_tensor("p_dbg", [128, 4 * KVN], BF16,
                               kind="ExternalOutput").ap()
    q_h_d = nc.dram_tensor("q_h", [D, T], BF16, kind=SCR).ap()
    q_l_d = nc.dram_tensor("q_l", [D, T], BF16, kind=SCR).ap()
    h1_h_d = nc.dram_tensor("h1_h", [D, T], BF16, kind=SCR).ap()
    h1_l_d = nc.dram_tensor("h1_l", [D, T], BF16, kind=SCR).ap()
    ah_h_d = nc.dram_tensor("ah_h", [D, T], BF16, kind=SCR).ap()
    ah_l_d = nc.dram_tensor("ah_l", [D, T], BF16, kind=SCR).ap()
    wupT_h_d = nc.dram_tensor("wupT_h", [NIC, NFC, 128, 128], BF16, kind="Internal").ap()
    wupT_l_d = nc.dram_tensor("wupT_l", [NIC, NFC, 128, 128], BF16, kind="Internal").ap()
    wgT_h_d = nc.dram_tensor("wgT_h", [NIC, NFC, 128, 128], BF16, kind="Internal").ap()
    wgT_l_d = nc.dram_tensor("wgT_l", [NIC, NFC, 128, 128], BF16, kind="Internal").ap()
    wdT_h_d = nc.dram_tensor("wdT_h", [NIC, NFC, 128, 128], BF16, kind="Internal").ap()
    wdT_l_d = nc.dram_tensor("wdT_l", [NIC, NFC, 128, 128], BF16, kind="Internal").ap()

    id_bf_d = nc.inline_tensor(np.eye(128, dtype=ml_dtypes.bfloat16), name="id_bf").ap()
    id_f32_d = nc.inline_tensor(np.eye(128, dtype=np.float32), name="id_f32").ap()
    h128_d = nc.inline_tensor(hadamard(128).astype(ml_dtypes.bfloat16), name="h128").ap()

    P = Prog()
    cv = Carver(nc)

    xnh = cv.tensor(1, "xnh", [128, NFC * T], BF16, 0)
    xnl = cv.tensor(1, "xnl", [128, NFC * T], BF16, 65536)
    ph = cv.tensor(1, "ph", [128, 4 * KVN], BF16, 0)
    pl = cv.tensor(1, "pl", [128, 4 * KVN], BF16, 32768)
    p32 = cv.tensor(1, "p32", [128, KVN], F32, 65536)
    kfm_h = cv.tensor(1, "kfm_h", [128, 2 * KVN], BF16, 81920)
    kfm_l = cv.tensor(1, "kfm_l", [128, 2 * KVN], BF16, 98304)
    qh_b = cv.tensor(1, "qh_b", [128, 4 * 256], BF16, 114688)
    ql_b = cv.tensor(1, "ql_b", [128, 4 * 256], BF16, 116736)
    ahst_h = cv.tensor(1, "ahst_h", [128, 2 * 256], BF16, 118784)
    ahst_l = cv.tensor(1, "ahst_l", [128, 2 * 256], BF16, 119808)
    pt = cv.tensor(1, "pt", [128, 4 * 1024], BF16, 120832)
    woTb_h = cv.tensor(1, "woTb_h", [128, 2 * D], BF16, 0)
    woTb_l = cv.tensor(1, "woTb_l", [128, 2 * D], BF16, 8192)
    absb_h = cv.tensor(1, "absb_h", [128, 2 * NFC * 512], BF16, 16384)
    absb_l = cv.tensor(1, "absb_l", [128, 2 * NFC * 512], BF16, 49152)
    wrow5 = cv.tensor(1, "wrow5", [128, 2 * D], F32, 81920)
    wsp5_h = cv.tensor(1, "wsp5_h", [128, 2 * D], BF16, 98304)
    wsp5_l = cv.tensor(1, "wsp5_l", [128, 2 * D], BF16, 106496)
    evst5 = cv.tensor(1, "evst5", [128, 512], F32, 114688)
    ostg5 = cv.tensor(1, "ostg5", [128, 2 * 512], F32, 116736)
    z1 = cv.tensor(1, "z1", [128, NIC * 512], F32, 0)

    p_f = cv.tensor(3, "p_f", [128, 6 * 512], F32, 0, space="PSUM")
    p_tr_a = cv.tensor(3, "p_tr_a", [128, 1024], BF16, 12288, space="PSUM")
    p_tr_b = cv.tensor(3, "p_tr_b", [128, 1024], BF16, 14336, space="PSUM")
    p_t32 = cv.tensor(3, "p_t32", [128, 512], F32, 14336, space="PSUM")

    xtm = cv.tensor(2, "xtm", [128, 2 * D], F32, 0)
    xn32 = cv.tensor(2, "xn32", [128, 2 * D], F32, 16384)
    xnh_tm = cv.tensor(2, "xnh_tm", [128, 2 * D], BF16, 32768)
    xnl_tm = cv.tensor(2, "xnl_tm", [128, 2 * D], BF16, 40960)
    sdump = cv.tensor(2, "sdump", [128, D], BF16, 49152)
    wrow = cv.tensor(2, "wrow", [128, 2 * D], F32, 0)
    wsp_h = cv.tensor(2, "wsp_h", [128, 2 * D], BF16, 16384)
    wsp_l = cv.tensor(2, "wsp_l", [128, 2 * D], BF16, 24576)
    wstage = cv.tensor(2, "wstage", [128, 2 * D], BF16, 32768)
    wqTb_h = cv.tensor(2, "wqTb_h", [128, 2 * D], BF16, 40960)
    wqTb_l = cv.tensor(2, "wqTb_l", [128, 2 * D], BF16, 49152)
    qst_h = cv.tensor(2, "qst_h", [128, T], BF16, 57344)
    qst_l = cv.tensor(2, "qst_l", [128, T], BF16, 61440)
    f1 = cv.tensor(2, "f1", [128, NFC * 512], F32, 32768)
    f1h = cv.tensor(2, "f1h", [128, NFC * 512], BF16, 0)
    f1l = cv.tensor(2, "f1l", [128, NFC * 512], BF16, 16384)
    kvt32 = cv.tensor(2, "kvt32", [128, 2 * 1024], F32, 40960)
    kvt_h = cv.tensor(2, "kvt_h", [128, 2 * 1024], BF16, 49152)
    kvt_l = cv.tensor(2, "kvt_l", [128, 2 * 1024], BF16, 53248)
    vtm_h = cv.tensor(2, "vtm_h", [128, KVN], BF16, 57344)
    vtm_l = cv.tensor(2, "vtm_l", [128, KVN], BF16, 65536)
    h1h_s = cv.tensor(2, "h1h_s", [128, NFC * 512], BF16, 0)
    h1l_s = cv.tensor(2, "h1l_s", [128, NFC * 512], BF16, 16384)
    wug = cv.tensor(2, "wug", [128, 4 * D], BF16, 32768)
    wdb = cv.tensor(2, "wdb", [128, 2 * 2 * 768], BF16, 49152)
    z2h = cv.tensor(2, "z2h", [128, 2 * 512], BF16, 55296)
    z2l = cv.tensor(2, "z2l", [128, 2 * 512], BF16, 57344)
    silu_b = cv.tensor(2, "silu_b", [128, 512], F32, 59392)
    a32 = cv.tensor(2, "a32", [128, 512], F32, 61440)
    evst6 = cv.tensor(2, "evst6", [128, 512], F32, 63488)
    ostg6 = cv.tensor(2, "ostg6", [128, 2 * 512], F32, 65536)
    a_h_t = cv.tensor(2, "a_h_t", [128, 2 * 512], BF16, 69632)
    a_l_t = cv.tensor(2, "a_l_t", [128, 2 * 512], BF16, 71680)

    ctx = contextlib.ExitStack()
    with ctx:
        reals = {
            1: ctx.enter_context(nc.sbuf_tensor("AR1", [128, 131072], U8)),
            2: ctx.enter_context(nc.sbuf_tensor("AR2", [128, 73728], U8)),
            3: ctx.enter_context(nc.psum_tensor("PSA", [128, 4096], F32)),
        }
        cv.materialize(reals)
        sid_bf = _ap_of(ctx.enter_context(nc.sbuf_tensor("sid_bf", [128, 128], BF16)))
        hsu_r = _ap_of(ctx.enter_context(nc.sbuf_tensor("hsu_r", [128, 256], BF16)))
        sid_f32 = _ap_of(ctx.enter_context(nc.sbuf_tensor("sid_f32", [128, 128], F32)))
        sh128 = _ap_of(ctx.enter_context(nc.sbuf_tensor("sh128", [128, 128], BF16)))
        sconst = _ap_of(ctx.enter_context(nc.sbuf_tensor("sconst", [128, 8], F32)))
        svec = _ap_of(ctx.enter_context(nc.sbuf_tensor("svec", [128, 224], F32)))
        stat = _ap_of(ctx.enter_context(nc.sbuf_tensor("stat", [128, 16], F32)))
        smax = _ap_of(ctx.enter_context(nc.sbuf_tensor("smax", [128, 24], F32)))
        p_tr = p_tr_a
        sem = None

        def fb(i, lo=0, n=512):
            if i == 6:
                return p_t32[:, lo:lo + n]
            return p_f[:, i * 512 + lo:i * 512 + lo + n]

        st_tr = {"eng": None, "val": 0}
        st_trb = {"eng": None, "val": 0}

        def tr_wait():
            return [(st_tr["eng"], st_tr["val"])] if st_tr["eng"] else []

        def tr_done(eng, val):
            st_tr["eng"], st_tr["val"] = eng, val

        def trb_wait():
            return [(st_trb["eng"], st_trb["val"])] if st_trb["eng"] else []

        def trb_done(eng, val):
            st_trb["eng"], st_trb["val"] = eng, val

        st_t6 = {"eng": None, "val": 0}

        def t6_wait():
            return [(st_t6["eng"], st_t6["val"])] if st_t6["eng"] else []

        def t6_done(eng, val):
            st_t6["eng"], st_t6["val"] = eng, val

        P.emit("sync", _dma(sid_bf[:, :], id_bf_d), inc="cL", inc_by=16)
        P.emit("sync", _dma(sid_f32[:, :], id_f32_d), inc="cL", inc_by=16)
        P.emit("sync", _dma(sh128[:, :], h128_d), inc="cL", inc_by=16)
        P.emit("sync", _dma_nc(svec[:, 0:16], suu_d.rearrange("(c p) -> p c", p=128)),
               inc="cL", inc_by=16)
        P.emit("sync", _dma_nc(svec[:, 16:80], wsu_d[0:II].rearrange("(c p) -> p c", p=128)),
               inc="cL", inc_by=16)
        P.emit("sync", _dma_nc(svec[:, 80:144],
                            wsu_d[II:2 * II].rearrange("(c p) -> p c", p=128)),
               inc="cL", inc_by=16)
        P.emit("sync", _dma_nc(svec[:, 144:160], wsd_d.rearrange("(c p) -> p c", p=128)),
               inc="cL", inc_by=16)
        P.emit("sync", _dma_nc(svec[:, 160:224], sud_d.rearrange("(c p) -> p c", p=128)),
               inc="cL", inc_by=16)
        ld0 = P.now("cL")
        for i, c in enumerate((1.0 / 2048.0, LN_EPS, 2.0, C_FWHT1, -1.0,
                               SCALE_ATT, 1.0, 0.0)):
            P.emit("vector", _memset(sconst[:, i:i + 1], c), inc="VE")
        P.emit("vector", _tsm(svec[:, 16:80], svec[:, 16:80], 64.0),
               wait=[("cL", ld0)], inc="VE")
        P.emit("vector", _tsm(svec[:, 80:144], svec[:, 80:144], 64.0), inc="VE")
        P.emit("vector", _tsm(svec[:, 144:160], svec[:, 144:160],
                              64.0 * C_FWHT2), inc="VE")

        half_free = [[], []]
        for tc in range(NTC):
            hf = tc % 2
            xs = xtm[:, hf * D:(hf + 1) * D]
            ns = xn32[:, hf * D:(hf + 1) * D]
            nh = xnh_tm[:, hf * D:(hf + 1) * D]
            nl = xnl_tm[:, hf * D:(hf + 1) * D]
            xsem = "xL%d" % hf
            ld = P.emit("sync", _dma(xs, x_d[tc * 128:(tc + 1) * 128, :]),
                        wait=half_free[hf], inc=xsem, inc_by=16)
            gx = P.emit("gpsimd", _dma(out_d[tc * 128:(tc + 1) * 128, :], xs),
                        wait=[(xsem, ld)], inc="xS", inc_by=16)
            c = hf * 8
            sv = P.emit("vector", _rsum(stat[:, c + 0:c + 1], xs),
                        wait=[(xsem, ld)], inc="VE")
            ss = P.emit("scalar", _act(sdump[:, :], xs, AF.Square,
                                       acc=stat[:, c + 1:c + 2]),
                        wait=[(xsem, ld)], inc="SE")
            vm_ = P.emit("vector", _tsm(stat[:, c + 2:c + 3], stat[:, c + 0:c + 1],
                                        sconst[:, 0:1]),
                         wait=[("VE", sv), ("SE", ss)], inc="VE")
            sq_ = P.emit("scalar", _act(stat[:, c + 3:c + 4],
                                        stat[:, c + 2:c + 3], AF.Square),
                         wait=[("VE", vm_)], inc="SE")
            vvar = P.emit("vector", _stt(stat[:, c + 4:c + 5], stat[:, c + 1:c + 2],
                                         sconst[:, 0:1], stat[:, c + 3:c + 4],
                                         ALU.mult, ALU.subtract),
                          wait=[("SE", sq_)], inc="VE")
            srt = P.emit("scalar", _act(stat[:, c + 5:c + 6], stat[:, c + 4:c + 5],
                                        AF.Sqrt, bias=sconst[:, 1:2]),
                         wait=[("VE", vvar)], inc="SE")
            P.emit("vector", _tsm(stat[:, c + 7:c + 8], stat[:, c + 2:c + 3],
                                  sconst[:, 4:5]), inc="VE")
            P.emit("vector", _recip(stat[:, c + 6:c + 7], stat[:, c + 5:c + 6]),
                   wait=[("SE", srt)], inc="VE")
            P.emit("vector", _cp(stat[:, c + 5:c + 6], sconst[:, 6:7]), inc="VE")
            vxn = P.emit("vector", _ts(ns, xs, stat[:, c + 7:c + 8],
                                       stat[:, c + 6:c + 7], ALU.add, ALU.mult),
                         inc="VE")
            if DEBUG:
                P.emit("gpsimd", _dma(st_dbg[tc, :, :], stat[:, c:c + 8]),
                       wait=[("VE", vxn)], inc="dS", inc_by=16)
            shh = P.emit("scalar", _scp(nh, ns), wait=[("VE", vxn)], inc="SE")
            gll = P.emit("gpsimd", _sub(nl, ns, nh), wait=[("SE", shh)], inc="GE")
            half_free[hf] = [("GE", gll), ("xS", gx)]
            for part, (src, dst, pw) in enumerate(
                    ((nh, xnh, ("SE", shh)), (nl, xnl, ("GE", gll)))):
                for b in range(2):
                    tr = None
                    for j in range(8):
                        fc = b * 8 + j
                        wv = ([pw] + tr_wait()) if j == 0 else []
                        tr = P.emit("tensor",
                                    _tr(p_tr[:, j * 128:(j + 1) * 128],
                                        src[:, fc * 128:(fc + 1) * 128],
                                        sid_bf[:, :], j == 0, j == 7),
                                    wait=wv, inc="PE" if j == 7 else None)
                    dst3 = dst.rearrange("p (fc t) -> p fc t", t=T)[
                        :, b * 8:(b + 1) * 8, tc * 128:(tc + 1) * 128]
                    src3 = p_tr.rearrange("p (j d) -> p j d", d=128)
                    ev = P.emit("vector", _cp(dst3, src3), wait=[("PE", tr)],
                                inc="VE")
                    tr_done("VE", ev)
        xn_ve = P.now("VE")
        if DEBUG:
            P.emit("gpsimd", _dma(xnh_dbg, xnh[:, :]),
                   wait=[("VE", xn_ve), ("GE", P.now("GE"))], inc="dS",
                   inc_by=16)
            P.emit("gpsimd", _dma(xnl_dbg, xnl[:, :]), inc="dS", inc_by=16)

        if STAGE < 1:
            P.emit("gpsimd", _memset(smax[:, 15:16], 0.0),
                   wait=P.fence("xS", "qS", "fS", "wS", "aS", "o5S", "o6S",
                                "dS"),
                   inc="GE")
            _finish(nc, ctx, P)
            return nc
        p0_done = P.fence()
        wq_free = [list(p0_done), list(p0_done)]
        wqTb_rd = {}
        qst_free = []
        bankq_free = {0: [], 1: []}
        for oc in range(NFC):
            hf = oc % 2
            dh = wqTb_h[:, hf * D:(hf + 1) * D]
            dl = wqTb_l[:, hf * D:(hf + 1) * D]
            ws = wrow[:, hf * D:(hf + 1) * D]
            sh_b = wsp_h[:, hf * D:(hf + 1) * D]
            sl_b = wsp_l[:, hf * D:(hf + 1) * D]
            wsem = "wL%d" % hf
            ld = P.emit("sync", _dma(ws, wq_d[oc * 128:(oc + 1) * 128, :]),
                        wait=wq_free[hf], inc=wsem, inc_by=16)
            gh = P.emit("gpsimd", _cp(sh_b, ws), wait=[(wsem, ld)], inc="GE")
            gl = P.emit("gpsimd", _sub(sl_b, ws, sh_b), inc="GE")
            wq_free[hf] = [("GE", gl)]
            ev = None
            for part, (sb, dst) in enumerate(((sh_b, dh), (sl_b, dl))):
                for b in range(2):
                    tr = None
                    for j in range(8):
                        k = b * 8 + j
                        wv = ([("GE", gl)] + tr_wait()) if j == 0 else []
                        tr = P.emit("tensor",
                                    _tr(p_tr[:, j * 128:(j + 1) * 128],
                                        sb[:, k * 128:(k + 1) * 128],
                                        sid_bf[:, :], j == 0, j == 7),
                                    wait=wv, inc="PE" if j == 7 else None)
                    dst3 = dst[:, b * 1024:(b + 1) * 1024].rearrange(
                        "p (j d) -> p j d", d=128)
                    src3 = p_tr.rearrange("p (j d) -> p j d", d=128)
                    ev = P.emit("vector", _cp(dst3, src3), wait=[("PE", tr)],
                                inc="VE")
                    tr_done("VE", ev)
            for tt in range(4):
                bank = (oc * 4 + tt) % 2
                dst = fb(bank)
                mm = None
                for kc in range(NFC):
                    for pi, (wop, xop) in enumerate(((dh, xnh), (dl, xnh),
                                                     (dh, xnl))):
                        wv = []
                        if kc == 0 and pi == 0:
                            wv = [("VE", ev), ("VE", xn_ve)] + bankq_free[bank]
                        mm = P.emit("tensor", _mm(
                            dst, wop[:, kc * 128:(kc + 1) * 128],
                            xop[:, kc * T + tt * 512:kc * T + (tt + 1) * 512],
                            kc == 0 and pi == 0, kc == NFC - 1 and pi == 2),
                            wait=wv,
                            inc="PE" if (kc == NFC - 1 and pi == 2) else None)
                sh2 = P.emit("scalar", _scp(qst_h[:, tt * 512:(tt + 1) * 512], dst),
                             wait=[("PE", mm)] + qst_free, inc="SE")
                vl2 = P.emit("vector", _stt(qst_l[:, tt * 512:(tt + 1) * 512], dst,
                                            sconst[:, 6:7],
                                            qst_h[:, tt * 512:(tt + 1) * 512],
                                            ALU.mult, ALU.subtract),
                             wait=[("SE", sh2)], inc="VE")
                bankq_free[bank] = [("SE", sh2), ("VE", vl2)]
            wqTb_rd[oc] = P.now("PE")
            g1 = P.emit("gpsimd", _dma(q_h_d[oc * 128:(oc + 1) * 128, :],
                                       qst_h[:, :]),
                        wait=[("SE", P.now("SE")), ("VE", P.now("VE"))],
                        inc="qS", inc_by=16)
            g2 = P.emit("gpsimd", _dma(q_l_d[oc * 128:(oc + 1) * 128, :],
                                       qst_l[:, :]),
                        inc="qS", inc_by=16)
            qst_free = [("qS", g2)]

        if STAGE < 2:
            P.emit("gpsimd", _memset(smax[:, 15:16], 0.0),
                   wait=P.fence("xS", "qS", "fS", "wS", "aS", "o5S", "o6S",
                                "dS"),
                   inc="GE")
            _finish(nc, ctx, P)
            return nc
        p1_done = P.fence()
        hsu_used_pe = {}
        bankf_free = {0: bankq_free[0], 1: bankq_free[1]}
        f1_dma_free = []
        for tt in range(4):
            last_ev = None
            for fc in range(NFC):
                fcp = fc % 2
                hr = hsu_r[:, fcp * 128:(fcp + 1) * 128]
                wv = [("cL", ld0)]
                if (tt, fc) != (0, 0) and (fc >= 2 or tt > 0):
                    wv.append(("PE", hsu_used_pe.get((tt, fc - 2),
                                                     hsu_used_pe.get((tt - 1, NFC - 2 + fc), P.now("PE")))))
                hv = P.emit("vector", _tsm(hr, sh128[:, :], svec[:, fc:fc + 1]),
                            wait=wv, inc="VE")
                bank = fcp
                dst = fb(bank)
                wv0 = [("VE", hv)] + bankf_free[bank] + \
                    (p1_done if tt == 0 and fc < 2 else [])
                if tt == 0 and fc < 2:
                    wv0.append(("VE", xn_ve))
                P.emit("tensor", _mm(
                    dst, hr, xnh[:, fc * T + tt * 512:fc * T + (tt + 1) * 512],
                    True, False), wait=wv0, inc=None)
                m2 = P.emit("tensor", _mm(
                    dst, hr, xnl[:, fc * T + tt * 512:fc * T + (tt + 1) * 512],
                    False, True), inc="PE")
                hsu_used_pe[(tt, fc)] = m2
                last_ev = P.emit("scalar", _scp(f1[:, fc * 512:(fc + 1) * 512], dst),
                                 wait=[("PE", m2)] +
                                 (f1_dma_free if fc < 2 and tt > 0 else []),
                                 inc="SE")
                bankf_free[bank] = [("SE", last_ev)]
            bv = None
            for hstep in (1, 2, 4, 8):
                for cc in range(NFC):
                    if cc & hstep:
                        continue
                    a = f1[:, cc * 512:(cc + 1) * 512]
                    b = f1[:, (cc + hstep) * 512:(cc + hstep + 1) * 512]
                    wv = [("SE", last_ev)] if hstep == 1 and cc == 0 else []
                    P.emit("vector", _sub(b, a, b), wait=wv, inc=None)
                    bv = P.emit("vector", _stt(a, a, sconst[:, 2:3], b,
                                               ALU.mult, ALU.subtract), inc="VE")
            sh3 = P.emit("scalar", _act(f1h[:, :], f1[:, :], AF.Copy,
                                        scale=sconst[:, 3:4]),
                         wait=[("VE", bv)] + f1_dma_free, inc="SE")
            vl3 = P.emit("vector", _stt(f1l[:, :], f1[:, :], sconst[:, 3:4],
                                        f1h[:, :], ALU.mult, ALU.subtract),
                         wait=[("SE", sh3)], inc="VE")
            g2 = None
            for half, (sb, dd) in enumerate(((f1h, h1_h_d), (f1l, h1_l_d))):
                src3 = sb.rearrange("p (fc t) -> p fc t", t=512)
                dst3 = dd.rearrange("(fc p) t -> p fc t", p=128)[
                    :, :, tt * 512:(tt + 1) * 512]
                g2 = P.emit("gpsimd", _dma(dst3, src3),
                            wait=[("SE", sh3), ("VE", vl3)], inc="fS", inc_by=16)
            f1_dma_free = [("fS", g2)]
        p2_done = P.fence()

        if STAGE < 4:
            P.emit("gpsimd", _memset(smax[:, 15:16], 0.0),
                   wait=P.fence("xS", "qS", "fS", "wS", "aS", "o5S", "o6S",
                                "dS"),
                   inc="GE")
            _finish(nc, ctx, P)
            return nc
        p3_units = []
        for rb in range(NIC):
            p3_units.append(("up", rb, None))
            p3_units.append(("g", rb, None))
        for rb in range(NFC):
            for q in range(4):
                p3_units.append(("d", rb, q))
        p3_state = {"i": 0, "wrow_free": [list(p2_done), list(p2_done)],
                    "wst_free": [list(p2_done), list(p2_done)]}

        def emit_p3(count):
            srcs = {"up": (wup_d, wupT_h_d, wupT_l_d),
                    "g": (wg_d, wgT_h_d, wgT_l_d),
                    "d": (wd_d, wdT_h_d, wdT_l_d)}
            for _ in range(count):
                i = p3_state["i"]
                if i >= len(p3_units):
                    return
                p3_state["i"] = i + 1
                kind, rb, q = p3_units[i]
                src_d, dst_h4, dst_l4 = srcs[kind]
                hf = i % 2
                if kind == "d":
                    src = src_d[rb * 128:(rb + 1) * 128, q * 2048:(q + 1) * 2048]
                else:
                    src = src_d[rb * 128:(rb + 1) * 128, :]
                ws = wrow[:, hf * D:(hf + 1) * D]
                sh_b = wsp_h[:, hf * D:(hf + 1) * D]
                sl_b = wsp_l[:, hf * D:(hf + 1) * D]
                wsem = "wL%d" % hf
                ld = P.emit("sync", _dma(ws, src), wait=p3_state["wrow_free"][hf],
                            inc=wsem, inc_by=16)
                sh3_ = P.emit("scalar", _scp(sh_b, ws), wait=[(wsem, ld)],
                              inc="SE")
                gl = P.emit("gpsimd", _sub(sl_b, ws, sh_b),
                            wait=[("SE", sh3_)], inc="GE")
                p3_state["wrow_free"][hf] = [("GE", gl)]
                for part, (sb, dst4) in enumerate(((sh_b, dst_h4), (sl_b, dst_l4))):
                    sthf = (2 * i + part) % 2
                    wst = wstage[:, sthf * D:(sthf + 1) * D]
                    ev = None
                    for b in range(2):
                        tr = None
                        for j in range(8):
                            k = b * 8 + j
                            wv = []
                            if j == 0:
                                wv = [("GE", gl)] + tr_wait()
                                if b == 0:
                                    wv += p3_state["wst_free"][sthf]
                            tr = P.emit("tensor",
                                        _tr(p_tr[:, j * 128:(j + 1) * 128],
                                            sb[:, k * 128:(k + 1) * 128],
                                            sid_bf[:, :], j == 0, j == 7),
                                        wait=wv, inc="PE" if j == 7 else None)
                        dst3 = wst[:, b * 1024:(b + 1) * 1024].rearrange(
                            "p (k d) -> p k d", d=128)
                        src3 = p_tr.rearrange("p (j d) -> p j d", d=128)
                        ev = P.emit("vector", _cp(dst3, src3), wait=[("PE", tr)],
                                    inc="VE")
                        tr_done("VE", ev)
                    if kind == "d":
                        dd = dst4[q * 16:(q + 1) * 16, rb, :, :].rearrange(
                            "ic p j -> p ic j")
                    else:
                        dd = dst4[rb, :, :, :].rearrange("kc p j -> p kc j")
                    src3 = wst.rearrange("p (k d) -> p k d", d=128)
                    st = P.emit("gpsimd", _dma(dd, src3), wait=[("VE", ev)],
                                inc="wS", inc_by=16)
                    p3_state["wst_free"][sthf] = [("wS", st)]

        ks = {"ktr_pe": {}, "ld_free": [list(p2_done), list(p2_done)],
              "ksplit": {}, "ktr_ve": {}, "vsplit": {}, "qld": {},
              "kfm_rd": {}, "pv_done": {0: 0, 1: 0}, "scores_done": {},
              "q_slot_free": {}, "sc_grp": {}}
        score_free = {0: list(p2_done), 1: list(p2_done)}
        pv_free = {0: [], 1: []}
        pt_free = [[], []]
        ahst_free = [[], []]
        p32_free = []
        prev_exp_last = {}
        pend_pv = []

        def emit_k_group(h, g):
            gf = g % 2
            k3 = kvk_d.rearrange("(kc p) d -> p kc d", p=128)[
                :, g * 8:(g + 1) * 8, h * 128:(h + 1) * 128]
            dstk = kvt32[:, gf * 1024:(gf + 1) * 1024]
            dh = kvt_h[:, gf * 1024:(gf + 1) * 1024]
            dl = kvt_l[:, gf * 1024:(gf + 1) * 1024]
            ksem = "kL%d" % gf
            ld = P.emit("sync", _dma(
                dstk.rearrange("p (kc d) -> p kc d", d=128), k3),
                wait=ks["ld_free"][gf], inc=ksem, inc_by=16)
            wk0 = P.fence("SE", "fS", "qS") if h == 0 else []
            if (h, g - 2) in ks["ktr_pe"]:
                wk0 = wk0 + [("PE", ks["ktr_pe"][(h, g - 2)])]
            sh_ = P.emit("scalar", _scp(dh, dstk), wait=[(ksem, ld)] + wk0,
                         inc="SE")
            gl = P.emit("gpsimd", _sub(dl, dstk, dh), wait=[("SE", sh_)],
                        inc="GE")
            ks["ld_free"][gf] = [("GE", gl)]
            ks["ksplit"][(h, g)] = (sh_, gl)

        def emit_v_prep(h):
            wv0 = [("PE", ks["pv_done_all"])] if h > 0 else \
                P.fence("SE", "fS", "qS")
            for g in range(4):
                gf = g % 2
                v3 = kvv_d.rearrange("(kc p) d -> p kc d", p=128)[
                    :, g * 8:(g + 1) * 8, h * 128:(h + 1) * 128]
                dstk = kvt32[:, gf * 1024:(gf + 1) * 1024]
                ksem = "kL%d" % gf
                ld = P.emit("sync", _dma(
                    dstk.rearrange("p (kc d) -> p kc d", d=128), v3),
                    wait=ks["ld_free"][gf], inc=ksem, inc_by=16)
                sh_ = P.emit("scalar", _scp(vtm_h[:, g * 1024:(g + 1) * 1024],
                                            dstk),
                             wait=[(ksem, ld)] + (wv0 if g == 0 else []),
                             inc="SE")
                gl = P.emit("gpsimd", _sub(vtm_l[:, g * 1024:(g + 1) * 1024],
                                           dstk,
                                           vtm_h[:, g * 1024:(g + 1) * 1024]),
                            wait=[("SE", sh_)], inc="GE")
                ks["ld_free"][gf] = [("GE", gl)]
            ks["vsplit"][h] = ("SE", P.now("SE"), P.now("GE"))

        def emit_q_load(g):
            h, qb = g // 8, g % 8
            slot = g % 4
            qsem = "qL%d" % slot
            wv = list(ks["q_slot_free"].get(slot, []))
            if g == 0:
                wv += [("qS", P.now("qS")), ("PE", P.now("PE"))]
            P.emit("sync", _dma(
                qh_b[:, slot * 256:(slot + 1) * 256],
                q_h_d[h * 128:(h + 1) * 128, qb * 256:(qb + 1) * 256]),
                wait=wv, inc=qsem, inc_by=16)
            q2 = P.emit("sync", _dma(
                ql_b[:, slot * 256:(slot + 1) * 256],
                q_l_d[h * 128:(h + 1) * 128, qb * 256:(qb + 1) * 256]),
                inc=qsem, inc_by=16)
            ks["qld"][g] = (qsem, q2)

        def emit_ktr_group_one(h, g):
            hp = h % 2
            gf = g % 2
            sh_, gl = ks["ksplit"][(h, g)]
            wv0 = [("PE", ks["kfm_rd"][h - 2])] if h >= 2 and ks["kfm_rd"][h - 2] else []
            for part, (sb, dst) in enumerate(((kvt_h, kfm_h), (kvt_l, kfm_l))):
                tr = None
                for j in range(8):
                    wv = []
                    if j == 0:
                        wv = [("SE", sh_), ("GE", gl)] + tr_wait() + wv0
                    tr = P.emit("tensor", _tr(
                        p_tr[:, j * 128:(j + 1) * 128],
                        sb[:, gf * 1024 + j * 128:gf * 1024 + (j + 1) * 128],
                        sid_bf[:, :], j == 0, j == 7),
                        wait=wv, inc="PE" if j == 7 else None)
                dst2 = dst[:, hp * KVN + g * 1024:hp * KVN + (g + 1) * 1024]
                dst3 = dst2.rearrange("p (j d) -> p j d", d=128)
                src3 = p_tr.rearrange("p (j d) -> p j d", d=128)
                ev = P.emit("scalar", _scp(dst3, src3), wait=[("PE", tr)],
                            inc="SE")
                tr_done("SE", ev)
            ks["ktr_ve"][h] = ("SE", P.now("SE"))

        emit_k_group(0, 0)
        emit_k_group(0, 1)
        emit_ktr_group_one(0, 0)
        emit_k_group(0, 2)
        emit_ktr_group_one(0, 1)
        emit_k_group(0, 3)
        emit_ktr_group_one(0, 2)
        emit_ktr_group_one(0, 3)
        emit_v_prep(0)
        for g0 in range(3):
            emit_q_load(g0)

        for h in range(HEADS):
            hp = h % 2
            kread = ks["ktr_ve"][h]
            vspl = ks["vsplit"][h]
            pv_group_pe = {}
            for qb in range(8):
                qbp = qb % 2
                gq = h * 8 + qb
                qld = ks["qld"][gq]
                qrs = (gq % 4) * 256
                sc_pe_start = P.now("PE")
                spl = {}
                for qcx in range(2):
                    if qcx == 1 and pend_pv:
                        emit_pv_part(pend_pv[0], 0, 8)
                    qc = qb * 2 + qcx
                    qcp = qc % 2
                    qslot = qbp * 2 + qcx
                    qsl = qh_b[:, qrs + qcx * 128:qrs + (qcx + 1) * 128]
                    qsll = ql_b[:, qrs + qcx * 128:qrs + (qcx + 1) * 128]
                    sm = 9 * qcp
                    exps = []
                    vm2 = None
                    for kb in range(4):
                        kbp = kb % 2
                        reg = p_f[:, (2 * kbp) * 512:(2 * kbp + 2) * 512]
                        mm = None
                        for kt in range(2):
                            dstp = reg[:, kt * 512:(kt + 1) * 512]
                            for pi, (qop, kop) in enumerate(
                                    ((qsl, kfm_h), (qsll, kfm_h), (qsl, kfm_l))):
                                wv = []
                                if kt == 0 and pi == 0:
                                    wv = score_free[kbp] + [kread, qld]
                                rhs = kop[:, hp * KVN + kb * 1024 + kt * 512:
                                          hp * KVN + kb * 1024 + (kt + 1) * 512]
                                mm = P.emit("tensor",
                                            _mm(dstp, qop, rhs, pi == 0, pi == 2),
                                            wait=wv,
                                            inc="PE" if (kt == 1 and pi == 2)
                                            else None)
                        if kb == 0:
                            wvm = [("PE", mm)]
                            if prev_exp_last.get(qcp):
                                wvm.append(("SE", prev_exp_last[qcp]))
                            P.emit("vector", _rmax(smax[:, sm:sm + 1], reg, True),
                                   wait=wvm, inc="VE")
                            P.emit("vector", _cp(smax[:, 20:21], sconst[:, 6:7]),
                                   inc="VE")
                            vm2 = P.emit("vector",
                                         _tsm(smax[:, sm + 1:sm + 2],
                                              smax[:, sm:sm + 1], sconst[:, 5:6]),
                                         inc="VE")
                        ex = P.emit("scalar", _act(
                            p32[:, kb * 1024:(kb + 1) * 1024], reg, AF.Exp,
                            bias=smax[:, sm + 1:sm + 2], scale=sconst[:, 5:6],
                            acc=smax[:, sm + 2 + kb:sm + 3 + kb]),
                            wait=[("PE", mm), ("VE", vm2)] + p32_free,
                            inc="SE")
                        exps.append(ex)
                        score_free[kbp] = [("SE", ex)]
                    prev_exp_last[qcp] = exps[-1]
                    P.emit("vector", _add(smax[:, sm + 6:sm + 7],
                                          smax[:, sm + 2:sm + 3],
                                          smax[:, sm + 3:sm + 4]),
                           wait=[("SE", exps[-1])], inc="VE")
                    P.emit("vector", _add(smax[:, sm + 7:sm + 8],
                                          smax[:, sm + 4:sm + 5],
                                          smax[:, sm + 5:sm + 6]), inc="VE")
                    P.emit("vector", _cp(smax[:, 20:21], sconst[:, 6:7]),
                           inc="VE")
                    P.emit("vector", _add(smax[:, sm + 2:sm + 3],
                                          smax[:, sm + 6:sm + 7],
                                          smax[:, sm + 7:sm + 8]), inc="VE")
                    P.emit("vector", _cp(smax[:, 20:21], sconst[:, 6:7]),
                           inc="VE")
                    vr = P.emit("vector", _recip(smax[:, sm + 8:sm + 9],
                                                 smax[:, sm + 2:sm + 3]),
                                inc="VE")
                    P.emit("vector", _cp(smax[:, 20:21], sconst[:, 6:7]),
                           inc="VE")
                    phs = ph[:, qslot * KVN:(qslot + 1) * KVN]
                    pls = pl[:, qslot * KVN:(qslot + 1) * KVN]
                    wv = [("VE", vr)]
                    if ks["pv_done"][qbp]:
                        wv.append(("PE", ks["pv_done"][qbp]))
                    vpl = None
                    for qt in range(2):
                        qs_ = slice(qt * 2048, (qt + 1) * 2048)
                        P.emit("vector", _tsm(phs[:, qs_], p32[:, qs_],
                                              smax[:, sm + 8:sm + 9]),
                               wait=(wv if qt == 0 else []), inc="VE")
                        vpl = P.emit("vector", _stt(pls[:, qs_], p32[:, qs_],
                                                    smax[:, sm + 8:sm + 9],
                                                    phs[:, qs_],
                                                    ALU.mult, ALU.subtract),
                                     inc="VE")
                    p32_free = [("VE", vpl)]
                    spl[qcx] = vpl

                def emit_pv_part(st, lo, hi):
                    h, qb, spl0, spl1, vspl = st["args"]
                    qbp = qb % 2
                    pvdst = fb(4 + qbp, 0, 256)
                    pend_mm = st["pend_mm"]

                    def pv_tr(kg):
                        kgp = kg % 2
                        kgs = kg % 4
                        ptr = p_tr if kgp == 0 else p_tr_b
                        ptrw = tr_wait if kgp == 0 else trb_wait
                        ptrd = tr_done if kgp == 0 else trb_done
                        tr = None
                        for kcg in range(2):
                            kc = kg * 2 + kcg
                            for part, psrc in enumerate((ph, pl)):
                                for qcx in range(2):
                                    qslot = qbp * 2 + qcx
                                    j = kcg * 4 + part * 2 + qcx
                                    wv = []
                                    if j == 0 and kcg == 0:
                                        wv = [("VE", spl0), ("VE", spl1)] + \
                                            ptrw() + pt_free[kgp]
                                    tr = P.emit("tensor", _tr(
                                        ptr[:, j * 128:(j + 1) * 128],
                                        psrc[:, qslot * KVN + kc * 128:
                                             qslot * KVN + (kc + 1) * 128],
                                        sid_bf[:, :], j == 0, j == 7),
                                        wait=wv, inc="PE" if j == 7 else None)
                        ptd = pt[:, kgs * 1024:(kgs + 1) * 1024]
                        pev = P.emit("scalar", _scp(ptd, ptr[:, :]),
                                     wait=[("PE", tr)], inc="SE")
                        psem = "SE"
                        ptrd(psem, pev)
                        return kg, kgp, ptd, psem, pev

                    def pv_mm(kg, kgp, ptd, psem, pev):
                        for kcg in range(2):
                            kc = kg * 2 + kcg
                            vh_t = vtm_h[:, kc * 128:(kc + 1) * 128]
                            vl_t = vtm_l[:, kc * 128:(kc + 1) * 128]
                            rh = ptd[:, kcg * 512:kcg * 512 + 256]
                            rl = ptd[:, kcg * 512 + 256:kcg * 512 + 512]
                            for pi, (vop, rop) in enumerate(
                                    ((vh_t, rh), (vl_t, rh), (vh_t, rl))):
                                wv = [(psem, pev)]
                                if kg == 0 and kcg == 0 and pi == 0:
                                    wv += [("SE", vspl[1]), ("GE", vspl[2])] + \
                                        pv_free[qbp]
                                mm = P.emit("tensor", _mm(
                                    pvdst, vop, rop,
                                    kg == 0 and kcg == 0 and pi == 0,
                                    kg == 15 and kcg == 1 and pi == 2),
                                    wait=wv,
                                    inc="PE" if (kcg == 1 and pi == 2) else None)
                        pt_free[kgp] = [("PE", mm)]
                        st["mm"] = mm

                    for kg in range(lo, hi):
                        trs = pv_tr(kg)
                        if len(pend_mm) >= 3:
                            pv_mm(*pend_mm.pop(0))
                        pend_mm.append(trs)
                    if hi == 16:
                        while pend_mm:
                            pv_mm(*pend_mm.pop(0))
                        emit_pv_finish(st)

                def emit_pv_finish(st):
                    h, qb, spl0, spl1, vspl = st["args"]
                    qbp = qb % 2
                    mm = st["mm"]
                    ks["pv_done"][qbp] = mm
                    ap_h = ahst_h[:, qbp * 256:(qbp + 1) * 256]
                    ap_l = ahst_l[:, qbp * 256:(qbp + 1) * 256]
                    pvrd = fb(4 + qbp, 0, 256)
                    se1 = P.emit("scalar", _scp(ap_h, pvrd),
                                 wait=[("PE", mm)] + ahst_free[qbp], inc="SE")
                    ve1 = P.emit("vector", _stt(ap_l, pvrd, sconst[:, 6:7],
                                                ap_h, ALU.mult, ALU.subtract),
                                 wait=[("SE", se1)], inc="VE")
                    pv_free[qbp] = [("SE", se1), ("VE", ve1)]
                    g1 = P.emit("gpsimd", _dma(
                        ah_h_d[h * 128:(h + 1) * 128, qb * 256:(qb + 1) * 256],
                        ap_h), wait=[("SE", se1)], inc="aS", inc_by=16)
                    g2 = P.emit("gpsimd", _dma(
                        ah_l_d[h * 128:(h + 1) * 128, qb * 256:(qb + 1) * 256],
                        ap_l), wait=[("VE", ve1)], inc="aS", inc_by=16)
                    ahst_free[qbp] = [("aS", g2)]

                if pend_pv:
                    emit_pv_part(pend_pv[0], 8, 16)
                    pend_pv.pop(0)
                ks["sc_grp"][gq] = P.now("PE")
                ks["q_slot_free"][gq % 4] = [("PE", ks["sc_grp"][gq])]
                if gq + 3 < HEADS * 8:
                    emit_q_load(gq + 3)
                if DEBUG and h == 0 and qb == 0:
                    P.emit("gpsimd", _dma(kfm_dbg[:, 0:KVN], kfm_h[:, 0:KVN]),
                           wait=[kread], inc="dS", inc_by=16)
                    P.emit("gpsimd", _dma(kfm_dbg[:, KVN:2 * KVN],
                                          kfm_l[:, 0:KVN]), inc="dS", inc_by=16)
                    P.emit("gpsimd", _dma(vtm_dbg[:, 0:KVN], vtm_h[:, :]),
                           wait=[("SE", vspl[1]), ("GE", vspl[2])],
                           inc="dS", inc_by=16)
                    P.emit("gpsimd", _dma(vtm_dbg[:, KVN:2 * KVN], vtm_l[:, :]),
                           inc="dS", inc_by=16)
                    P.emit("gpsimd", _dma(p_dbg[:, 0:KVN], ph[:, 0:KVN]),
                           wait=[("VE", spl[1])], inc="dS", inc_by=16)
                    P.emit("gpsimd", _dma(p_dbg[:, KVN:2 * KVN], pl[:, 0:KVN]),
                           inc="dS", inc_by=16)
                    P.emit("gpsimd", _dma(p_dbg[:, 2 * KVN:3 * KVN],
                                          ph[:, KVN:2 * KVN]), inc="dS", inc_by=16)
                    P.emit("gpsimd", _dma(p_dbg[:, 3 * KVN:4 * KVN],
                                          pl[:, KVN:2 * KVN]), inc="dS", inc_by=16)
                pend_pv.append({"args": (h, qb, spl[0], spl[1], vspl),
                                "pend_mm": [], "mm": None})
                if h + 1 < HEADS and qb < 4:
                    emit_k_group(h + 1, qb)
                    if qb >= 1:
                        emit_ktr_group_one(h + 1, qb - 1)
                if h + 1 < HEADS and qb == 4:
                    emit_ktr_group_one(h + 1, 3)
                if 1 <= qb <= 6:
                    emit_p3(2)
            ks["scores_done"][h] = P.now("PE")
            while pend_pv:
                emit_pv_part(pend_pv[0], 0, 16)
                pend_pv.pop(0)
            ks["pv_done_all"] = P.now("PE")
            ks["kfm_rd"][h] = P.now("PE")
            if h + 1 < HEADS:
                emit_v_prep(h + 1)
        emit_p3(len(p3_units))
        p4_done = P.fence()

        if STAGE < 5:
            P.emit("gpsimd", _memset(smax[:, 15:16], 0.0),
                   wait=P.fence("xS", "qS", "fS", "wS", "aS", "o5S", "o6S",
                                "dS"),
                   inc="GE")
            _finish(nc, ctx, P)
            return nc
        wo_free = [list(p4_done), list(p4_done)]
        ab_free = [[], []]
        ostg5_free = [[], []]
        evst5_free = []
        bank5_free = {0: [], 1: []}
        woTb_rd = {}
        for oc in range(NFC):
            hf = oc % 2
            dh = woTb_h[:, hf * D:(hf + 1) * D]
            dl = woTb_l[:, hf * D:(hf + 1) * D]
            ws = wrow5[:, hf * D:(hf + 1) * D]
            sh_b = wsp5_h[:, hf * D:(hf + 1) * D]
            sl_b = wsp5_l[:, hf * D:(hf + 1) * D]
            osem = "oL%d" % hf
            ld = P.emit("sync", _dma(ws, wo_d[oc * 128:(oc + 1) * 128, :]),
                        wait=wo_free[hf], inc=osem, inc_by=16)
            gh = P.emit("gpsimd", _cp(sh_b, ws),
                        wait=[(osem, ld)] + (p4_done if oc < 2 else []), inc="GE")
            gl = P.emit("gpsimd", _sub(sl_b, ws, sh_b), inc="GE")
            wo_free[hf] = [("GE", gl)]
            ev = None
            for part, (sb, dst) in enumerate(((sh_b, dh), (sl_b, dl))):
                for b in range(2):
                    tr = None
                    for j in range(8):
                        k = b * 8 + j
                        wv = []
                        if j == 0:
                            wv = [("GE", gl)] + tr_wait()
                            if oc >= 2 and b == 0 and part == 0:
                                wv.append(("PE", woTb_rd[oc - 2]))
                        tr = P.emit("tensor",
                                    _tr(p_tr[:, j * 128:(j + 1) * 128],
                                        sb[:, k * 128:(k + 1) * 128],
                                        sid_bf[:, :], j == 0, j == 7),
                                    wait=wv, inc="PE" if j == 7 else None)
                    dst3 = dst[:, b * 1024:(b + 1) * 1024].rearrange(
                        "p (j d) -> p j d", d=128)
                    src3 = p_tr.rearrange("p (j d) -> p j d", d=128)
                    ev = P.emit("vector", _cp(dst3, src3), wait=[("PE", tr)],
                                inc="VE")
                    tr_done("VE", ev)
            for tt in range(4):
                ttf = tt % 2
                lds = []
                for half, (dd, dstb) in enumerate(((ah_h_d, absb_h),
                                                   (ah_l_d, absb_l))):
                    src3 = dd.rearrange("(kc p) t -> p kc t", p=128)[
                        :, :, tt * 512:(tt + 1) * 512]
                    dst2 = dstb[:, ttf * NFC * 512:(ttf + 1) * NFC * 512]
                    dst3 = dst2.rearrange("p (kc t) -> p kc t", t=512)
                    wv = list(ab_free[ttf]) + (p4_done if oc == 0 and tt < 2 else [])
                    lds.append(P.emit("sync", _dma(dst3, src3), wait=wv,
                                      inc="aL%d" % ttf, inc_by=16))
                bank = (oc * 4 + tt) % 2
                dstp = fb(bank)
                abh = absb_h[:, ttf * NFC * 512:(ttf + 1) * NFC * 512]
                abl = absb_l[:, ttf * NFC * 512:(ttf + 1) * NFC * 512]
                mm = None
                for kc in range(NFC):
                    for pi, (wop, aop) in enumerate(((dh, abh), (dl, abh),
                                                     (dh, abl))):
                        wv = []
                        if kc == 0 and pi == 0:
                            wv = [("VE", ev), ("aL%d" % ttf, lds[1])] + bank5_free[bank]
                        mm = P.emit("tensor", _mm(
                            dstp, wop[:, kc * 128:(kc + 1) * 128],
                            aop[:, kc * 512:(kc + 1) * 512],
                            kc == 0 and pi == 0, kc == NFC - 1 and pi == 2),
                            wait=wv,
                            inc="PE" if kc == NFC - 1 and pi == 2 else None)
                ab_free[ttf] = [("PE", mm)]
                se1 = P.emit("scalar", _scp(evst5[:, :], dstp),
                             wait=[("PE", mm)] + evst5_free, inc="SE")
                bank5_free[bank] = [("SE", se1)]
                trl = None
                for j in range(4):
                    wv = []
                    if j == 0:
                        wv = [("SE", se1)] + t6_wait()
                    trl = P.emit("tensor", _tr(fb(6, j * 128, 128),
                                               evst5[:, j * 128:(j + 1) * 128],
                                               sid_f32[:, :], j == 0, j == 3),
                                 wait=wv, inc="PE" if j == 3 else None)
                ost = ostg5[:, ttf * 512:(ttf + 1) * 512]
                ev2 = P.emit("vector", _cp(ost, fb(6)),
                             wait=[("PE", trl)] + ostg5_free[ttf], inc="VE")
                t6_done("VE", ev2)
                evst5_free = [("VE", ev2)]
                dst3 = out_d.rearrange("(tc p) d -> p tc d", p=128)[
                    :, tt * 4:(tt + 1) * 4, oc * 128:(oc + 1) * 128]
                src3 = ost.rearrange("p (tc d) -> p tc d", d=128)
                g1 = P.emit("gpsimd", _dma_acc(dst3, src3), wait=[("VE", ev2)],
                            inc="o5S", inc_by=16)
                ostg5_free[ttf] = [("o5S", g1)]
            woTb_rd[oc] = P.now("PE")
        p5_done = P.fence()

        if STAGE < 6:
            P.emit("gpsimd", _memset(smax[:, 15:16], 0.0),
                   wait=P.fence("xS", "qS", "fS", "wS", "aS", "o5S", "o6S",
                                "dS"),
                   inc="GE")
            _finish(nc, ctx, P)
            return nc
        down_bank_evac = {}
        h1_free = []
        bu_free = {0: list(p5_done), 1: []}
        bg_free = {0: [], 1: []}
        bz_free = {0: [], 1: []}
        wug_u_free = []
        wug_g_free = []
        wdb_free = [list(p5_done), list(p5_done)]
        z2_free = [[], []]
        evst6_free = []
        ostg6_free = [[], []]
        hsu6_pe = {}
        pend_z = []
        bf_lvl = {}
        bf_bv = {"v": 0}
        OCG = ((0, 6), (6, 12), (12, 16))

        def bf_cascade(newtile):
            bf_lvl[newtile] = 0
            first_wait = [("SE", z1_ev[newtile])]
            while True:
                fired = False
                for l in range(6):
                    hstep = 1 << l
                    for c in range(NIC):
                        if c & hstep:
                            continue
                        if bf_lvl.get(c) == l and bf_lvl.get(c + hstep) == l:
                            a = z1[:, c * 512:(c + 1) * 512]
                            b = z1[:, (c + hstep) * 512:(c + hstep + 1) * 512]
                            P.emit("vector", _sub(b, a, b), wait=first_wait,
                                   inc=None)
                            bf_bv["v"] = P.emit("vector",
                                                _stt(a, a, sconst[:, 2:3], b,
                                                     ALU.mult, ALU.subtract),
                                                inc="VE")
                            first_wait = []
                            bf_lvl[c] = l + 1
                            bf_lvl[c + hstep] = l + 1
                            fired = True
                if not fired:
                    break
        for tt in range(4):
            lh = []
            for half, (dd, dstb) in enumerate(((h1_h_d, h1h_s), (h1_l_d, h1l_s))):
                src3 = dd.rearrange("(kc p) t -> p kc t", p=128)[
                    :, :, tt * 512:(tt + 1) * 512]
                dst3 = dstb.rearrange("p (kc t) -> p kc t", t=512)
                lh.append(P.emit("sync", _dma(dst3, src3),
                                 wait=(p5_done if tt == 0 else []) + h1_free,
                                 inc="hL", inc_by=16))
            h1lod = lh[-1]
            z1_ev = {}
            bf_lvl.clear()
            bf_bv["v"] = 0
            for ic in range(NIC):
                icp = ic % 2
                ldu1 = P.emit("sync", _dma(
                    wug[:, 0:2048].rearrange("p (kc j) -> p kc j", j=128),
                    wupT_h_d[ic, :, :, :].rearrange("kc p j -> p kc j")),
                    wait=wug_u_free + (p5_done if tt == 0 and ic == 0 else []),
                    inc="uL", inc_by=16)
                ldu = P.emit("sync", _dma(
                    wug[:, 2048:4096].rearrange("p (kc j) -> p kc j", j=128),
                    wupT_l_d[ic, :, :, :].rearrange("kc p j -> p kc j")),
                    inc="uL", inc_by=16)
                bu = fb(0 if icp == 0 else 2)
                mmu = None
                for kc in range(NFC):
                    for pi in range(3):
                        wof = 0 if pi != 1 else 2048
                        xop = h1h_s if pi != 2 else h1l_s
                        wop = wug[:, wof + kc * 128:wof + (kc + 1) * 128]
                        wv = []
                        if kc == 0 and pi == 0:
                            wv = [("uL", ldu), ("hL", h1lod)] + bu_free[icp]
                        mmu = P.emit("tensor", _mm(
                            bu, wop, xop[:, kc * 512:(kc + 1) * 512],
                            kc == 0 and pi == 0, kc == NFC - 1 and pi == 2),
                            wait=wv,
                            inc="PE" if kc == NFC - 1 and pi == 2 else None)
                wug_u_free = [("PE", mmu)]
                ldg1 = P.emit("sync", _dma(
                    wug[:, 4096:6144].rearrange("p (kc j) -> p kc j", j=128),
                    wgT_h_d[ic, :, :, :].rearrange("kc p j -> p kc j")),
                    wait=wug_g_free + (p5_done if tt == 0 and ic == 0 else []),
                    inc="gL", inc_by=16)
                ldg = P.emit("sync", _dma(
                    wug[:, 6144:8192].rearrange("p (kc j) -> p kc j", j=128),
                    wgT_l_d[ic, :, :, :].rearrange("kc p j -> p kc j")),
                    inc="gL", inc_by=16)
                bg = fb(1 if icp == 0 else 3)
                mmg = None
                for kc in range(NFC):
                    for pi in range(3):
                        wof = 4096 if pi != 1 else 6144
                        xop = h1h_s if pi != 2 else h1l_s
                        wop = wug[:, wof + kc * 128:wof + (kc + 1) * 128]
                        wv = []
                        if kc == 0 and pi == 0:
                            wv = [("gL", ldg)] + bg_free[icp]
                        mmg = P.emit("tensor", _mm(
                            bg, wop, xop[:, kc * 512:(kc + 1) * 512],
                            kc == 0 and pi == 0, kc == NFC - 1 and pi == 2),
                            wait=wv,
                            inc="PE" if kc == NFC - 1 and pi == 2 else None)
                wug_g_free = [("PE", mmg)]
                se1 = P.emit("scalar", _act(silu_b[:, :], bg, AF.Silu,
                                            scale=svec[:, 80 + ic:81 + ic]),
                             wait=[("PE", mmg)], inc="SE")
                bg_free[icp] = [("SE", se1)]
                va = P.emit("vector", _stt(a32[:, :], bu,
                                           svec[:, 16 + ic:17 + ic],
                                           silu_b[:, :], ALU.mult, ALU.mult),
                            wait=[("SE", se1), ("PE", mmu)], inc="VE")
                bu_free[icp] = [("VE", va)]
                aht = a_h_t[:, icp * 512:(icp + 1) * 512]
                alt = a_l_t[:, icp * 512:(icp + 1) * 512]
                wvs = [("VE", va)]
                if hsu6_pe.get((tt, ic - 1)):
                    wvs.append(("PE", hsu6_pe[(tt, ic - 1)]))
                sh1 = P.emit("scalar", _scp(aht, a32[:, :]),
                             wait=wvs, inc="SE")
                vl1 = P.emit("vector", _stt(alt, a32[:, :],
                                            sconst[:, 6:7], aht,
                                            ALU.mult, ALU.subtract),
                             wait=[("SE", sh1)], inc="VE")
                hr = hsu_r[:, icp * 128:(icp + 1) * 128]
                wvh = [("cL", ld0)]
                if tt == 0 and ic < 2:
                    wvh += p5_done
                if hsu6_pe.get((tt, ic - 2)) or (tt > 0 and ic < 2):
                    wvh.append(("PE", hsu6_pe.get((tt, ic - 2),
                                                  P.now("PE"))))
                hv = P.emit("vector", _tsm(hr, sh128[:, :],
                                           svec[:, 160 + ic:161 + ic]),
                            wait=wvh, inc="VE")

                def emit_zstage(ic, icp, hr, aht, alt, hv, sh1, vl1):
                    bz = fb(4 if icp == 0 else 5)
                    P.emit("tensor", _mm(bz, hr, aht, True, False),
                           wait=[("VE", hv), ("SE", sh1)] + bz_free[icp],
                           inc=None)
                    m2 = P.emit("tensor", _mm(bz, hr, alt, False, True),
                                wait=[("VE", vl1)], inc="PE")
                    hsu6_pe[(tt, ic)] = m2
                    ze = P.emit("scalar",
                                _scp(z1[:, ic * 512:(ic + 1) * 512], bz),
                                wait=[("PE", m2)], inc="SE")
                    bz_free[icp] = [("SE", ze)]
                    z1_ev[ic] = ze

                pend_z.append((ic, icp, hr, aht, alt, hv, sh1, vl1))
                if len(pend_z) > 1:
                    emit_zstage(*pend_z.pop(0))
                    bf_cascade(ic - 1)
            while pend_z:
                st_ = pend_z.pop(0)
                emit_zstage(*st_)
                bf_cascade(st_[0])
            assert all(bf_lvl[c] == 6 for c in range(NIC)), bf_lvl
            bv = bf_bv["v"]
            for gi, (o0, o1) in enumerate(OCG):
                nb = o1 - o0
                mm = None
                for ic in range(NIC):
                    icp = ic % 2
                    wdsl = wdb[:, icp * 1536:icp * 1536 + nb * 128]
                    wdsl_l = wdb[:, icp * 1536 + 768:icp * 1536 + 768 + nb * 128]
                    dsem = "dL%d" % icp
                    P.emit("sync", _dma(
                        wdsl.rearrange("p (o j) -> p o j", j=128),
                        wdT_h_d[ic, o0:o1, :, :].rearrange("o p j -> p o j")),
                        wait=wdb_free[icp], inc=dsem, inc_by=16)
                    ldd = P.emit("sync", _dma(
                        wdsl_l.rearrange("p (o j) -> p o j", j=128),
                        wdT_l_d[ic, o0:o1, :, :].rearrange("o p j -> p o j")),
                        inc=dsem, inc_by=16)
                    zh = z2h[:, icp * 512:(icp + 1) * 512]
                    zl = z2l[:, icp * 512:(icp + 1) * 512]
                    s1 = P.emit("scalar", _scp(zh, z1[:, ic * 512:(ic + 1) * 512]),
                                wait=[("VE", bv)] + z2_free[icp], inc="SE")
                    v1 = P.emit("vector", _stt(zl, z1[:, ic * 512:(ic + 1) * 512],
                                               sconst[:, 6:7], zh,
                                               ALU.mult, ALU.subtract),
                                wait=[("SE", s1)], inc="VE")
                    for oi in range(nb):
                        dstp = fb(oi)
                        for pi in range(3):
                            wop = (wdsl if pi != 1 else wdsl_l)[
                                :, oi * 128:(oi + 1) * 128]
                            rop = zh if pi != 2 else zl
                            wv = [(dsem, ldd), ("SE", s1)]
                            if pi == 2:
                                wv.append(("VE", v1))
                            if ic == 0 and pi == 0:
                                se_prev = down_bank_evac.get(oi)
                                if se_prev:
                                    wv.append(("SE", se_prev))
                            mm = P.emit("tensor", _mm(
                                dstp, wop, rop, ic == 0 and pi == 0,
                                ic == NIC - 1 and pi == 2),
                                wait=wv,
                                inc="PE" if (oi == nb - 1 and pi == 2)
                                else None)
                    z2_free[icp] = [("PE", mm)]
                    wdb_free[icp] = [("PE", mm)]
                dn_pe = P.now("PE")
                for oi in range(nb):
                    oc = o0 + oi
                    se2 = P.emit("scalar", _act(evst6[:, :], fb(oi), AF.Copy,
                                                scale=svec[:, 144 + oc:145 + oc]),
                                 wait=[("PE", dn_pe)] + evst6_free, inc="SE")
                    down_bank_evac[oi] = se2
                    trl = None
                    for j in range(4):
                        wv = []
                        if j == 0:
                            wv = [("SE", se2)] + t6_wait()
                        trl = P.emit("tensor", _tr(fb(6, j * 128, 128),
                                                   evst6[:, j * 128:(j + 1) * 128],
                                                   sid_f32[:, :], j == 0, j == 3),
                                     wait=wv, inc="PE" if j == 3 else None)
                    ost = ostg6[:, (oi % 2) * 512:(oi % 2 + 1) * 512]
                    ev2 = P.emit("vector", _cp(ost, fb(6)),
                                 wait=[("PE", trl)] + ostg6_free[oi % 2],
                                 inc="VE")
                    t6_done("VE", ev2)
                    evst6_free = [("VE", ev2)]
                    dst3 = out_d.rearrange("(tc p) d -> p tc d", p=128)[
                        :, tt * 4:(tt + 1) * 4, oc * 128:(oc + 1) * 128]
                    src3 = ost.rearrange("p (tc d) -> p tc d", d=128)
                    g1 = P.emit("gpsimd", _dma_acc(dst3, src3),
                                wait=[("VE", ev2)], inc="o6S", inc_by=16)
                    ostg6_free[oi % 2] = [("o6S", g1)]
            h1_free = [("PE", P.now("PE"))]
            end_f = [("SE", P.now("SE")), ("VE", P.now("VE"))]
            bu_free = {0: list(end_f), 1: list(end_f)}
            bg_free = {0: list(end_f), 1: list(end_f)}
            bz_free = {0: list(end_f), 1: list(end_f)}
        P.emit("gpsimd", _memset(smax[:, 15:16], 0.0),
               wait=P.fence("xS", "qS", "fS", "wS", "aS", "o5S", "o6S", "dS"),
               inc="GE")

        _finish(nc, ctx, P)
    return nc


_NC_CACHE = {}


def kernel(L0_attn_query, L0_attn_kv_k, L0_attn_kv_v, L0_attn_Wq, L0_attn_Wo,
           L0_ffn_SU_ug, L0_ffn_SU_dp, L0_ffn_Wscale_ug, L0_ffn_Wscale_dp,
           L0_ffn_W_up, L0_ffn_W_gate, L0_ffn_W_down):
    x = np.ascontiguousarray(np.asarray(L0_attn_query, dtype=np.float32)
                             .reshape(16384, D))
    shared = {
        "kvk": np.ascontiguousarray(np.asarray(L0_attn_kv_k, dtype=np.float32)
                                    .reshape(KVN, D)),
        "kvv": np.ascontiguousarray(np.asarray(L0_attn_kv_v, dtype=np.float32)
                                    .reshape(KVN, D)),
        "wq": np.ascontiguousarray(np.asarray(L0_attn_Wq, dtype=np.float32)),
        "wo": np.ascontiguousarray(np.asarray(L0_attn_Wo, dtype=np.float32)),
        "su_ug": np.ascontiguousarray(np.asarray(L0_ffn_SU_ug, dtype=np.float32)),
        "su_dp": np.ascontiguousarray(np.asarray(L0_ffn_SU_dp, dtype=np.float32)),
        "wscale_ug": np.ascontiguousarray(np.asarray(L0_ffn_Wscale_ug,
                                                     dtype=np.float32)),
        "wscale_dp": np.ascontiguousarray(np.asarray(L0_ffn_Wscale_dp,
                                                     dtype=np.float32)),
        "wup": np.ascontiguousarray(np.asarray(L0_ffn_W_up, dtype=np.float32)),
        "wg": np.ascontiguousarray(np.asarray(L0_ffn_W_gate, dtype=np.float32)),
        "wd": np.ascontiguousarray(np.asarray(L0_ffn_W_down, dtype=np.float32)),
    }
    if "nc" not in _NC_CACHE:
        _NC_CACHE["nc"] = build_nc()
    nc = _NC_CACHE["nc"]
    in_maps = []
    for c in range(8):
        m = dict(shared)
        m["x"] = np.ascontiguousarray(x[c * T:(c + 1) * T, :])
        in_maps.append(m)
    res = run_bass_kernel_spmd(nc, in_maps, core_ids=list(range(8)))
    out = np.concatenate([np.asarray(r["out"]) for r in res.results], axis=0)
    return out.reshape(1, 16384, D).astype(np.float32)


# version 08
# speedup vs baseline: 1.0721x; 1.0721x over previous
import contextlib

import numpy as np
import ml_dtypes
import concourse.bass as bass
import concourse.mybir as mybir
from concourse.bass_utils import run_bass_kernel_spmd

F32 = mybir.dt.float32
BF16 = mybir.dt.bfloat16
U8 = mybir.dt.uint8
AF = mybir.ActivationFunctionType
ALU = mybir.AluOpType
AX = mybir.AxisListType

T = 2048
D = 2048
HEADS = 16
KVN = 4096
II = 8192
NFC = D // 128
NIC = II // 128
NTC = T // 128
SCALE_ATT = float(1.0 / np.float32(np.sqrt(np.float32(128.0))))
C_FWHT1 = float(1.0 / (64.0 * np.sqrt(2048.0)))
C_FWHT2 = float(1.0 / (64.0 * np.sqrt(8192.0)))
LN_EPS = 1e-5

ENGINES = ("sync", "vector", "scalar", "tensor", "gpsimd")
STAGE = 6
DEBUG = False


def hadamard(n):
    h = np.array([[1.0]], dtype=np.float64)
    while h.shape[0] < n:
        h = np.block([[h, h], [h, -h]])
    return h


def _dma(dst, src):
    return lambda e: e.dma_start(out=dst, in_=src)


def _dma_nc(dst, src):
    return lambda e: e.dma_start(out=dst, in_=src,
                                 allow_slow_non_contiguous=True)


def _dma_acc(dst, src):
    return lambda e: e.dma_start(out=dst, in_=src, accum_op=ALU.add)


def _mm(dst, w, r, a0, st):
    return lambda e: e.matmul(dst, w, r, start=a0, stop=st)


def _tr(dst, src, ident, a0=True, st=True):
    return lambda e: e.matmul(dst, src, ident, start=a0, stop=st,
                              is_transpose=True)


def _cp(dst, src):
    return lambda e: e.tensor_copy(dst, src)


def _scp(dst, src):
    return lambda e: e.copy(dst, src)


def _act(dst, src, f, bias=0.0, scale=1.0, acc=None):
    return lambda e: e.activation(dst, src, f, bias=bias, scale=scale,
                                  accum_out=acc)


def _ts(dst, src, s1, s2, op0, op1):
    return lambda e: e.tensor_scalar(dst, src, s1, s2, op0, op1)


def _tsm(dst, src, s):
    return lambda e: e.tensor_scalar_mul(dst, src, s)


def _stt(dst, in0, s, in1, op0, op1):
    return lambda e: e.scalar_tensor_tensor(dst, in0, s, in1, op0, op1)


def _sub(dst, a, b):
    return lambda e: e.tensor_sub(dst, a, b)


def _add(dst, a, b):
    return lambda e: e.tensor_add(dst, a, b)


def _mul(dst, a, b):
    return lambda e: e.tensor_mul(dst, a, b)


def _rmax(dst, src, neg):
    return lambda e: e.reduce_max(dst, src, axis=AX.X, negate=neg)


def _rsum(dst, src):
    return lambda e: e.reduce_sum(dst, src, axis=AX.X)


def _recip(dst, src):
    return lambda e: e.reciprocal(dst, src)


def _memset(dst, c):
    return lambda e: e.memset(dst, c)


class Prog:
    def __init__(self, semnames=()):
        self.q = {e: [] for e in ENGINES}
        self.cnt = {}
        for s in semnames:
            self.cnt[s] = 0

    def emit(self, eng, fn, wait=(), inc=None, inc_by=1):
        wl = []
        for s, v in wait:
            assert v <= self.cnt.get(s, 0), (s, v, self.cnt.get(s, 0))
            if v > 0:
                wl.append((s, int(v)))
        r = None
        if inc is not None:
            self.cnt[inc] = self.cnt.get(inc, 0) + inc_by
            r = self.cnt[inc]
        self.q[eng].append((fn, tuple(wl), inc, inc_by))
        return r

    def now(self, sem):
        return self.cnt.get(sem, 0)

    def fence(self, *sems):
        if not sems:
            sems = tuple(self.cnt.keys())
        return [(s, self.cnt[s]) for s in sems if self.cnt.get(s, 0) > 0]

    def replay(self, block, sems):
        def mk(name):
            def body(eng):
                seen = {}
                for fn, waits, inc, inc_by in self.q[name]:
                    for s, val in waits:
                        if seen.get(s, 0) < val:
                            eng.wait_ge(sems[s], val)
                            seen[s] = val
                    ins = fn(eng)
                    if inc is not None:
                        ins.then_inc(sems[inc], inc_by)
            return body
        block.sync(mk("sync"))
        block.vector(mk("vector"))
        block.scalar(mk("scalar"))
        block.tensor(mk("tensor"))
        block.gpsimd(mk("gpsimd"))


class Carver:
    def __init__(self, nc):
        self.nc = nc
        self.items = []

    def tensor(self, arena_key, name, shape, dtype, offset, space="SB"):
        if space == "PSUM":
            h = self.nc.virtual_psum_tensor(name, shape, dtype)
        else:
            h = self.nc.virtual_sbuf_tensor(name, shape, dtype)
        per_part = int(np.prod(shape[1:])) * mybir.dt.size(dtype)
        self.items.append((arena_key, h, offset, offset + per_part, space))
        return h.ap()

    def materialize(self, reals):
        for key, h, off, end, space in self.items:
            real = reals[key]
            per_real = (int(np.prod(real.shape[1:])) *
                        mybir.dt.size(real.dtype))
            assert end <= per_real, (h.name, end, per_real)
            self.nc._tensor(h.name, list(h.shape), h.dtype, type=space)
            mloc = self.nc.lookup_mloc(h)
            mloc.allocated = not self.nc.target_bir_lowering
            mloc.addr = self.nc.lookup_mloc(real).addr + off


def _ap_of(h):
    return h.ap()


def _finish(nc, ctx, P):
    sems = {n: ctx.enter_context(nc.semaphore(n)) for n in sorted(P.cnt)}
    with nc.Block() as block:
        P.replay(block, sems)


def build_nc():
    nc = bass.Bass("TRN2", target_bir_lowering=False)

    x_d = nc.dram_tensor("x", [T, D], F32, kind="ExternalInput").ap()
    kvk_d = nc.dram_tensor("kvk", [KVN, D], F32, kind="ExternalInput").ap()
    kvv_d = nc.dram_tensor("kvv", [KVN, D], F32, kind="ExternalInput").ap()
    wq_d = nc.dram_tensor("wq", [D, D], F32, kind="ExternalInput").ap()
    wo_d = nc.dram_tensor("wo", [D, D], F32, kind="ExternalInput").ap()
    suu_d = nc.dram_tensor("su_ug", [D], F32, kind="ExternalInput").ap()
    sud_d = nc.dram_tensor("su_dp", [II], F32, kind="ExternalInput").ap()
    wsu_d = nc.dram_tensor("wscale_ug", [2 * II], F32, kind="ExternalInput").ap()
    wsd_d = nc.dram_tensor("wscale_dp", [D], F32, kind="ExternalInput").ap()
    wup_d = nc.dram_tensor("wup", [II, D], F32, kind="ExternalInput").ap()
    wg_d = nc.dram_tensor("wg", [II, D], F32, kind="ExternalInput").ap()
    wd_d = nc.dram_tensor("wd", [D, II], F32, kind="ExternalInput").ap()
    out_d = nc.dram_tensor("out", [T, D], F32, kind="ExternalOutput").ap()

    SCR = "ExternalOutput" if DEBUG else "Internal"
    if DEBUG:
        xnh_dbg = nc.dram_tensor("xnh_dbg", [128, NFC * T], BF16,
                                 kind="ExternalOutput").ap()
        xnl_dbg = nc.dram_tensor("xnl_dbg", [128, NFC * T], BF16,
                                 kind="ExternalOutput").ap()
        st_dbg = nc.dram_tensor("st_dbg", [NTC, 128, 8], F32,
                                 kind="ExternalOutput").ap()
        kfm_dbg = nc.dram_tensor("kfm_dbg", [128, 2 * KVN], BF16,
                                 kind="ExternalOutput").ap()
        vtm_dbg = nc.dram_tensor("vtm_dbg", [128, 2 * KVN], BF16,
                                 kind="ExternalOutput").ap()
        p_dbg = nc.dram_tensor("p_dbg", [128, 4 * KVN], BF16,
                               kind="ExternalOutput").ap()
    q_h_d = nc.dram_tensor("q_h", [D, T], BF16, kind=SCR).ap()
    q_l_d = nc.dram_tensor("q_l", [D, T], BF16, kind=SCR).ap()
    h1_h_d = nc.dram_tensor("h1_h", [D, T], BF16, kind=SCR).ap()
    h1_l_d = nc.dram_tensor("h1_l", [D, T], BF16, kind=SCR).ap()
    ah_h_d = nc.dram_tensor("ah_h", [D, T], BF16, kind=SCR).ap()
    ah_l_d = nc.dram_tensor("ah_l", [D, T], BF16, kind=SCR).ap()
    wupT_h_d = nc.dram_tensor("wupT_h", [NIC, NFC, 128, 128], BF16, kind="Internal").ap()
    wupT_l_d = nc.dram_tensor("wupT_l", [NIC, NFC, 128, 128], BF16, kind="Internal").ap()
    wgT_h_d = nc.dram_tensor("wgT_h", [NIC, NFC, 128, 128], BF16, kind="Internal").ap()
    wgT_l_d = nc.dram_tensor("wgT_l", [NIC, NFC, 128, 128], BF16, kind="Internal").ap()
    wdT_h_d = nc.dram_tensor("wdT_h", [NIC, NFC, 128, 128], BF16, kind="Internal").ap()
    wdT_l_d = nc.dram_tensor("wdT_l", [NIC, NFC, 128, 128], BF16, kind="Internal").ap()

    id_bf_d = nc.inline_tensor(np.eye(128, dtype=ml_dtypes.bfloat16), name="id_bf").ap()
    id_f32_d = nc.inline_tensor(np.eye(128, dtype=np.float32), name="id_f32").ap()
    h128_d = nc.inline_tensor(hadamard(128).astype(ml_dtypes.bfloat16), name="h128").ap()

    P = Prog()
    cv = Carver(nc)

    xnh = cv.tensor(1, "xnh", [128, NFC * T], BF16, 0)
    xnl = cv.tensor(1, "xnl", [128, NFC * T], BF16, 65536)
    ph = cv.tensor(1, "ph", [128, 4 * KVN], BF16, 0)
    pl = cv.tensor(1, "pl", [128, 4 * KVN], BF16, 32768)
    p32 = cv.tensor(1, "p32", [128, KVN], F32, 65536)
    kfm_h = cv.tensor(1, "kfm_h", [128, 2 * KVN], BF16, 81920)
    kfm_l = cv.tensor(1, "kfm_l", [128, 2 * KVN], BF16, 98304)
    qh_b = cv.tensor(1, "qh_b", [128, 4 * 256], BF16, 114688)
    ql_b = cv.tensor(1, "ql_b", [128, 4 * 256], BF16, 116736)
    ahst_h = cv.tensor(1, "ahst_h", [128, 2 * 256], BF16, 118784)
    ahst_l = cv.tensor(1, "ahst_l", [128, 2 * 256], BF16, 119808)
    pt = cv.tensor(1, "pt", [128, 4 * 1024], BF16, 120832)
    woTb_h = cv.tensor(1, "woTb_h", [128, 2 * D], BF16, 0)
    woTb_l = cv.tensor(1, "woTb_l", [128, 2 * D], BF16, 8192)
    absb_h = cv.tensor(1, "absb_h", [128, 2 * NFC * 512], BF16, 16384)
    absb_l = cv.tensor(1, "absb_l", [128, 2 * NFC * 512], BF16, 49152)
    wrow5 = cv.tensor(1, "wrow5", [128, 2 * D], F32, 81920)
    wsp5_h = cv.tensor(1, "wsp5_h", [128, 2 * D], BF16, 98304)
    wsp5_l = cv.tensor(1, "wsp5_l", [128, 2 * D], BF16, 106496)
    evst5 = cv.tensor(1, "evst5", [128, 512], F32, 114688)
    ostg5 = cv.tensor(1, "ostg5", [128, 2 * 512], F32, 116736)
    z1 = cv.tensor(1, "z1", [128, NIC * 512], F32, 0)

    p_f = cv.tensor(3, "p_f", [128, 6 * 512], F32, 0, space="PSUM")
    p_tr_a = cv.tensor(3, "p_tr_a", [128, 1024], BF16, 12288, space="PSUM")
    p_tr_b = cv.tensor(3, "p_tr_b", [128, 1024], BF16, 14336, space="PSUM")
    p_t32 = cv.tensor(3, "p_t32", [128, 512], F32, 14336, space="PSUM")

    xtm = cv.tensor(2, "xtm", [128, 2 * D], F32, 0)
    xn32 = cv.tensor(2, "xn32", [128, 2 * D], F32, 16384)
    xnh_tm = cv.tensor(2, "xnh_tm", [128, 2 * D], BF16, 32768)
    xnl_tm = cv.tensor(2, "xnl_tm", [128, 2 * D], BF16, 40960)
    sdump = cv.tensor(2, "sdump", [128, D], BF16, 49152)
    wrow = cv.tensor(2, "wrow", [128, 2 * D], F32, 0)
    wsp_h = cv.tensor(2, "wsp_h", [128, 2 * D], BF16, 16384)
    wsp_l = cv.tensor(2, "wsp_l", [128, 2 * D], BF16, 24576)
    wstage = cv.tensor(2, "wstage", [128, 2 * D], BF16, 32768)
    wqTb_h = cv.tensor(2, "wqTb_h", [128, 2 * D], BF16, 40960)
    wqTb_l = cv.tensor(2, "wqTb_l", [128, 2 * D], BF16, 49152)
    qst_h = cv.tensor(2, "qst_h", [128, T], BF16, 57344)
    qst_l = cv.tensor(2, "qst_l", [128, T], BF16, 61440)
    f1 = cv.tensor(2, "f1", [128, NFC * 512], F32, 32768)
    f1h = cv.tensor(2, "f1h", [128, NFC * 512], BF16, 0)
    f1l = cv.tensor(2, "f1l", [128, NFC * 512], BF16, 16384)
    kvt32 = cv.tensor(2, "kvt32", [128, 2 * 1024], F32, 40960)
    kvt_h = cv.tensor(2, "kvt_h", [128, 2 * 1024], BF16, 49152)
    kvt_l = cv.tensor(2, "kvt_l", [128, 2 * 1024], BF16, 53248)
    vtm_h = cv.tensor(2, "vtm_h", [128, KVN], BF16, 57344)
    vtm_l = cv.tensor(2, "vtm_l", [128, KVN], BF16, 65536)
    h1h_s = cv.tensor(2, "h1h_s", [128, NFC * 512], BF16, 0)
    h1l_s = cv.tensor(2, "h1l_s", [128, NFC * 512], BF16, 16384)
    wug = cv.tensor(2, "wug", [128, 4 * D], BF16, 32768)
    wdb = cv.tensor(2, "wdb", [128, 2 * 2 * 768], BF16, 49152)
    z2h = cv.tensor(2, "z2h", [128, 2 * 512], BF16, 55296)
    z2l = cv.tensor(2, "z2l", [128, 2 * 512], BF16, 57344)
    silu_b = cv.tensor(2, "silu_b", [128, 512], F32, 59392)
    a32 = cv.tensor(2, "a32", [128, 512], F32, 61440)
    evst6 = cv.tensor(2, "evst6", [128, 512], F32, 63488)
    ostg6 = cv.tensor(2, "ostg6", [128, 2 * 512], F32, 65536)
    a_h_t = cv.tensor(2, "a_h_t", [128, 2 * 512], BF16, 69632)
    a_l_t = cv.tensor(2, "a_l_t", [128, 2 * 512], BF16, 71680)

    ctx = contextlib.ExitStack()
    with ctx:
        reals = {
            1: ctx.enter_context(nc.sbuf_tensor("AR1", [128, 131072], U8)),
            2: ctx.enter_context(nc.sbuf_tensor("AR2", [128, 73728], U8)),
            3: ctx.enter_context(nc.psum_tensor("PSA", [128, 4096], F32)),
        }
        cv.materialize(reals)
        sid_bf = _ap_of(ctx.enter_context(nc.sbuf_tensor("sid_bf", [128, 128], BF16)))
        hsu_r = _ap_of(ctx.enter_context(nc.sbuf_tensor("hsu_r", [128, 256], BF16)))
        sid_f32 = _ap_of(ctx.enter_context(nc.sbuf_tensor("sid_f32", [128, 128], F32)))
        sh128 = _ap_of(ctx.enter_context(nc.sbuf_tensor("sh128", [128, 128], BF16)))
        sconst = _ap_of(ctx.enter_context(nc.sbuf_tensor("sconst", [128, 8], F32)))
        svec = _ap_of(ctx.enter_context(nc.sbuf_tensor("svec", [128, 224], F32)))
        stat = _ap_of(ctx.enter_context(nc.sbuf_tensor("stat", [128, 16], F32)))
        smax = _ap_of(ctx.enter_context(nc.sbuf_tensor("smax", [128, 24], F32)))
        p_tr = p_tr_a
        sem = None

        def fb(i, lo=0, n=512):
            if i == 6:
                return p_t32[:, lo:lo + n]
            return p_f[:, i * 512 + lo:i * 512 + lo + n]

        st_tr = {"eng": None, "val": 0}
        st_trb = {"eng": None, "val": 0}

        def tr_wait():
            return [(st_tr["eng"], st_tr["val"])] if st_tr["eng"] else []

        def tr_done(eng, val):
            st_tr["eng"], st_tr["val"] = eng, val

        def trb_wait():
            return [(st_trb["eng"], st_trb["val"])] if st_trb["eng"] else []

        def trb_done(eng, val):
            st_trb["eng"], st_trb["val"] = eng, val

        st_t6 = {"eng": None, "val": 0}

        def t6_wait():
            return [(st_t6["eng"], st_t6["val"])] if st_t6["eng"] else []

        def t6_done(eng, val):
            st_t6["eng"], st_t6["val"] = eng, val

        P.emit("sync", _dma(sid_bf[:, :], id_bf_d), inc="cL", inc_by=16)
        P.emit("sync", _dma(sid_f32[:, :], id_f32_d), inc="cL", inc_by=16)
        P.emit("sync", _dma(sh128[:, :], h128_d), inc="cL", inc_by=16)
        P.emit("sync", _dma_nc(svec[:, 0:16], suu_d.rearrange("(c p) -> p c", p=128)),
               inc="cL", inc_by=16)
        P.emit("sync", _dma_nc(svec[:, 16:80], wsu_d[0:II].rearrange("(c p) -> p c", p=128)),
               inc="cL", inc_by=16)
        P.emit("sync", _dma_nc(svec[:, 80:144],
                            wsu_d[II:2 * II].rearrange("(c p) -> p c", p=128)),
               inc="cL", inc_by=16)
        P.emit("sync", _dma_nc(svec[:, 144:160], wsd_d.rearrange("(c p) -> p c", p=128)),
               inc="cL", inc_by=16)
        P.emit("sync", _dma_nc(svec[:, 160:224], sud_d.rearrange("(c p) -> p c", p=128)),
               inc="cL", inc_by=16)
        ld0 = P.now("cL")
        for i, c in enumerate((1.0 / 2048.0, LN_EPS, 2.0, C_FWHT1, -1.0,
                               SCALE_ATT, 1.0, 0.0)):
            P.emit("vector", _memset(sconst[:, i:i + 1], c), inc="VE")
        P.emit("vector", _tsm(svec[:, 16:80], svec[:, 16:80], 64.0),
               wait=[("cL", ld0)], inc="VE")
        P.emit("vector", _tsm(svec[:, 80:144], svec[:, 80:144], 64.0), inc="VE")
        P.emit("vector", _tsm(svec[:, 144:160], svec[:, 144:160],
                              64.0 * C_FWHT2), inc="VE")

        half_free = [[], []]
        for tc in range(NTC):
            hf = tc % 2
            xs = xtm[:, hf * D:(hf + 1) * D]
            ns = xn32[:, hf * D:(hf + 1) * D]
            nh = xnh_tm[:, hf * D:(hf + 1) * D]
            nl = xnl_tm[:, hf * D:(hf + 1) * D]
            xsem = "xL%d" % hf
            ld = P.emit("sync", _dma(xs, x_d[tc * 128:(tc + 1) * 128, :]),
                        wait=half_free[hf], inc=xsem, inc_by=16)
            gx = P.emit("gpsimd", _dma(out_d[tc * 128:(tc + 1) * 128, :], xs),
                        wait=[(xsem, ld)], inc="xS", inc_by=16)
            c = hf * 8
            sv = P.emit("vector", _rsum(stat[:, c + 0:c + 1], xs),
                        wait=[(xsem, ld)], inc="VE")
            ss = P.emit("scalar", _act(sdump[:, :], xs, AF.Square,
                                       acc=stat[:, c + 1:c + 2]),
                        wait=[(xsem, ld)], inc="SE")
            vm_ = P.emit("vector", _tsm(stat[:, c + 2:c + 3], stat[:, c + 0:c + 1],
                                        sconst[:, 0:1]),
                         wait=[("VE", sv), ("SE", ss)], inc="VE")
            sq_ = P.emit("scalar", _act(stat[:, c + 3:c + 4],
                                        stat[:, c + 2:c + 3], AF.Square),
                         wait=[("VE", vm_)], inc="SE")
            vvar = P.emit("vector", _stt(stat[:, c + 4:c + 5], stat[:, c + 1:c + 2],
                                         sconst[:, 0:1], stat[:, c + 3:c + 4],
                                         ALU.mult, ALU.subtract),
                          wait=[("SE", sq_)], inc="VE")
            srt = P.emit("scalar", _act(stat[:, c + 5:c + 6], stat[:, c + 4:c + 5],
                                        AF.Sqrt, bias=sconst[:, 1:2]),
                         wait=[("VE", vvar)], inc="SE")
            P.emit("vector", _tsm(stat[:, c + 7:c + 8], stat[:, c + 2:c + 3],
                                  sconst[:, 4:5]), inc="VE")
            P.emit("vector", _recip(stat[:, c + 6:c + 7], stat[:, c + 5:c + 6]),
                   wait=[("SE", srt)], inc="VE")
            P.emit("vector", _cp(stat[:, c + 5:c + 6], sconst[:, 6:7]), inc="VE")
            vxn = P.emit("vector", _ts(ns, xs, stat[:, c + 7:c + 8],
                                       stat[:, c + 6:c + 7], ALU.add, ALU.mult),
                         inc="VE")
            if DEBUG:
                P.emit("gpsimd", _dma(st_dbg[tc, :, :], stat[:, c:c + 8]),
                       wait=[("VE", vxn)], inc="dS", inc_by=16)
            shh = P.emit("scalar", _scp(nh, ns), wait=[("VE", vxn)], inc="SE")
            gll = P.emit("gpsimd", _sub(nl, ns, nh), wait=[("SE", shh)], inc="GE")
            half_free[hf] = [("GE", gll), ("xS", gx)]
            for part, (src, dst, pw) in enumerate(
                    ((nh, xnh, ("SE", shh)), (nl, xnl, ("GE", gll)))):
                for b in range(2):
                    tr = None
                    for j in range(8):
                        fc = b * 8 + j
                        wv = ([pw] + tr_wait()) if j == 0 else []
                        tr = P.emit("tensor",
                                    _tr(p_tr[:, j * 128:(j + 1) * 128],
                                        src[:, fc * 128:(fc + 1) * 128],
                                        sid_bf[:, :], j == 0, j == 7),
                                    wait=wv, inc="PE" if j == 7 else None)
                    dst3 = dst.rearrange("p (fc t) -> p fc t", t=T)[
                        :, b * 8:(b + 1) * 8, tc * 128:(tc + 1) * 128]
                    src3 = p_tr.rearrange("p (j d) -> p j d", d=128)
                    ev = P.emit("vector", _cp(dst3, src3), wait=[("PE", tr)],
                                inc="VE")
                    tr_done("VE", ev)
        xn_ve = P.now("VE")
        if DEBUG:
            P.emit("gpsimd", _dma(xnh_dbg, xnh[:, :]),
                   wait=[("VE", xn_ve), ("GE", P.now("GE"))], inc="dS",
                   inc_by=16)
            P.emit("gpsimd", _dma(xnl_dbg, xnl[:, :]), inc="dS", inc_by=16)

        if STAGE < 1:
            P.emit("gpsimd", _memset(smax[:, 15:16], 0.0),
                   wait=P.fence("xS", "qS", "fS", "wS", "aS", "o5S", "o6S",
                                "dS"),
                   inc="GE")
            _finish(nc, ctx, P)
            return nc
        p0_done = P.fence()
        wq_free = [list(p0_done), list(p0_done)]
        wqTb_rd = {}
        qst_free = []
        bankq_free = {0: [], 1: []}
        for oc in range(NFC):
            hf = oc % 2
            dh = wqTb_h[:, hf * D:(hf + 1) * D]
            dl = wqTb_l[:, hf * D:(hf + 1) * D]
            ws = wrow[:, hf * D:(hf + 1) * D]
            sh_b = wsp_h[:, hf * D:(hf + 1) * D]
            sl_b = wsp_l[:, hf * D:(hf + 1) * D]
            wsem = "wL%d" % hf
            ld = P.emit("sync", _dma(ws, wq_d[oc * 128:(oc + 1) * 128, :]),
                        wait=wq_free[hf], inc=wsem, inc_by=16)
            gh = P.emit("gpsimd", _cp(sh_b, ws), wait=[(wsem, ld)], inc="GE")
            gl = P.emit("gpsimd", _sub(sl_b, ws, sh_b), inc="GE")
            wq_free[hf] = [("GE", gl)]
            ev = None
            for part, (sb, dst) in enumerate(((sh_b, dh), (sl_b, dl))):
                for b in range(2):
                    tr = None
                    for j in range(8):
                        k = b * 8 + j
                        wv = ([("GE", gl)] + tr_wait()) if j == 0 else []
                        tr = P.emit("tensor",
                                    _tr(p_tr[:, j * 128:(j + 1) * 128],
                                        sb[:, k * 128:(k + 1) * 128],
                                        sid_bf[:, :], j == 0, j == 7),
                                    wait=wv, inc="PE" if j == 7 else None)
                    dst3 = dst[:, b * 1024:(b + 1) * 1024].rearrange(
                        "p (j d) -> p j d", d=128)
                    src3 = p_tr.rearrange("p (j d) -> p j d", d=128)
                    ev = P.emit("vector", _cp(dst3, src3), wait=[("PE", tr)],
                                inc="VE")
                    tr_done("VE", ev)
            for tt in range(4):
                bank = (oc * 4 + tt) % 2
                dst = fb(bank)
                mm = None
                for kc in range(NFC):
                    for pi, (wop, xop) in enumerate(((dh, xnh), (dl, xnh),
                                                     (dh, xnl))):
                        wv = []
                        if kc == 0 and pi == 0:
                            wv = [("VE", ev), ("VE", xn_ve)] + bankq_free[bank]
                        mm = P.emit("tensor", _mm(
                            dst, wop[:, kc * 128:(kc + 1) * 128],
                            xop[:, kc * T + tt * 512:kc * T + (tt + 1) * 512],
                            kc == 0 and pi == 0, kc == NFC - 1 and pi == 2),
                            wait=wv,
                            inc="PE" if (kc == NFC - 1 and pi == 2) else None)
                sh2 = P.emit("scalar", _scp(qst_h[:, tt * 512:(tt + 1) * 512], dst),
                             wait=[("PE", mm)] + qst_free, inc="SE")
                vl2 = P.emit("vector", _stt(qst_l[:, tt * 512:(tt + 1) * 512], dst,
                                            sconst[:, 6:7],
                                            qst_h[:, tt * 512:(tt + 1) * 512],
                                            ALU.mult, ALU.subtract),
                             wait=[("SE", sh2)], inc="VE")
                bankq_free[bank] = [("SE", sh2), ("VE", vl2)]
            wqTb_rd[oc] = P.now("PE")
            g1 = P.emit("gpsimd", _dma(q_h_d[oc * 128:(oc + 1) * 128, :],
                                       qst_h[:, :]),
                        wait=[("SE", P.now("SE")), ("VE", P.now("VE"))],
                        inc="qS", inc_by=16)
            g2 = P.emit("gpsimd", _dma(q_l_d[oc * 128:(oc + 1) * 128, :],
                                       qst_l[:, :]),
                        inc="qS", inc_by=16)
            qst_free = [("qS", g2)]

        if STAGE < 2:
            P.emit("gpsimd", _memset(smax[:, 15:16], 0.0),
                   wait=P.fence("xS", "qS", "fS", "wS", "aS", "o5S", "o6S",
                                "dS"),
                   inc="GE")
            _finish(nc, ctx, P)
            return nc
        p1_done = P.fence()
        hsu_used_pe = {}
        bankf_free = {0: bankq_free[0], 1: bankq_free[1]}
        f1_dma_free = []
        for tt in range(4):
            last_ev = None
            for fc in range(NFC):
                fcp = fc % 2
                hr = hsu_r[:, fcp * 128:(fcp + 1) * 128]
                wv = [("cL", ld0)]
                if (tt, fc) != (0, 0) and (fc >= 2 or tt > 0):
                    wv.append(("PE", hsu_used_pe.get((tt, fc - 2),
                                                     hsu_used_pe.get((tt - 1, NFC - 2 + fc), P.now("PE")))))
                hv = P.emit("vector", _tsm(hr, sh128[:, :], svec[:, fc:fc + 1]),
                            wait=wv, inc="VE")
                bank = fcp
                dst = fb(bank)
                wv0 = [("VE", hv)] + bankf_free[bank] + \
                    (p1_done if tt == 0 and fc < 2 else [])
                if tt == 0 and fc < 2:
                    wv0.append(("VE", xn_ve))
                P.emit("tensor", _mm(
                    dst, hr, xnh[:, fc * T + tt * 512:fc * T + (tt + 1) * 512],
                    True, False), wait=wv0, inc=None)
                m2 = P.emit("tensor", _mm(
                    dst, hr, xnl[:, fc * T + tt * 512:fc * T + (tt + 1) * 512],
                    False, True), inc="PE")
                hsu_used_pe[(tt, fc)] = m2
                last_ev = P.emit("scalar", _scp(f1[:, fc * 512:(fc + 1) * 512], dst),
                                 wait=[("PE", m2)] +
                                 (f1_dma_free if fc < 2 and tt > 0 else []),
                                 inc="SE")
                bankf_free[bank] = [("SE", last_ev)]
            bv = None
            for hstep in (1, 2, 4, 8):
                for cc in range(NFC):
                    if cc & hstep:
                        continue
                    a = f1[:, cc * 512:(cc + 1) * 512]
                    b = f1[:, (cc + hstep) * 512:(cc + hstep + 1) * 512]
                    wv = [("SE", last_ev)] if hstep == 1 and cc == 0 else []
                    P.emit("vector", _sub(b, a, b), wait=wv, inc=None)
                    bv = P.emit("vector", _stt(a, a, sconst[:, 2:3], b,
                                               ALU.mult, ALU.subtract), inc="VE")
            sh3 = P.emit("scalar", _act(f1h[:, :], f1[:, :], AF.Copy,
                                        scale=sconst[:, 3:4]),
                         wait=[("VE", bv)] + f1_dma_free, inc="SE")
            vl3 = P.emit("vector", _stt(f1l[:, :], f1[:, :], sconst[:, 3:4],
                                        f1h[:, :], ALU.mult, ALU.subtract),
                         wait=[("SE", sh3)], inc="VE")
            g2 = None
            for half, (sb, dd) in enumerate(((f1h, h1_h_d), (f1l, h1_l_d))):
                src3 = sb.rearrange("p (fc t) -> p fc t", t=512)
                dst3 = dd.rearrange("(fc p) t -> p fc t", p=128)[
                    :, :, tt * 512:(tt + 1) * 512]
                g2 = P.emit("gpsimd", _dma(dst3, src3),
                            wait=[("SE", sh3), ("VE", vl3)], inc="fS", inc_by=16)
            f1_dma_free = [("fS", g2)]
        p2_done = P.fence()

        if STAGE < 4:
            P.emit("gpsimd", _memset(smax[:, 15:16], 0.0),
                   wait=P.fence("xS", "qS", "fS", "wS", "aS", "o5S", "o6S",
                                "dS"),
                   inc="GE")
            _finish(nc, ctx, P)
            return nc
        p3_units = []
        for rb in range(NIC):
            p3_units.append(("up", rb, None))
            p3_units.append(("g", rb, None))
        for rb in range(NFC):
            for q in range(4):
                p3_units.append(("d", rb, q))
        p3_state = {"i": 0, "wrow_free": [list(p2_done), list(p2_done)],
                    "wst_free": [list(p2_done), list(p2_done)]}

        def emit_p3(count):
            srcs = {"up": (wup_d, wupT_h_d, wupT_l_d),
                    "g": (wg_d, wgT_h_d, wgT_l_d),
                    "d": (wd_d, wdT_h_d, wdT_l_d)}
            for _ in range(count):
                i = p3_state["i"]
                if i >= len(p3_units):
                    return
                p3_state["i"] = i + 1
                kind, rb, q = p3_units[i]
                src_d, dst_h4, dst_l4 = srcs[kind]
                hf = i % 2
                if kind == "d":
                    src = src_d[rb * 128:(rb + 1) * 128, q * 2048:(q + 1) * 2048]
                else:
                    src = src_d[rb * 128:(rb + 1) * 128, :]
                ws = wrow[:, hf * D:(hf + 1) * D]
                sh_b = wsp_h[:, hf * D:(hf + 1) * D]
                sl_b = wsp_l[:, hf * D:(hf + 1) * D]
                wsem = "wL%d" % hf
                ld = P.emit("sync", _dma(ws, src), wait=p3_state["wrow_free"][hf],
                            inc=wsem, inc_by=16)
                sh3_ = P.emit("scalar", _scp(sh_b, ws), wait=[(wsem, ld)],
                              inc="SE")
                gl = P.emit("gpsimd", _sub(sl_b, ws, sh_b),
                            wait=[("SE", sh3_)], inc="GE")
                p3_state["wrow_free"][hf] = [("GE", gl)]
                for part, (sb, dst4) in enumerate(((sh_b, dst_h4), (sl_b, dst_l4))):
                    sthf = (2 * i + part) % 2
                    wst = wstage[:, sthf * D:(sthf + 1) * D]
                    ev = None
                    for b in range(2):
                        tr = None
                        for j in range(8):
                            k = b * 8 + j
                            wv = []
                            if j == 0:
                                wv = [("GE", gl)] + tr_wait()
                                if b == 0:
                                    wv += p3_state["wst_free"][sthf]
                            tr = P.emit("tensor",
                                        _tr(p_tr[:, j * 128:(j + 1) * 128],
                                            sb[:, k * 128:(k + 1) * 128],
                                            sid_bf[:, :], j == 0, j == 7),
                                        wait=wv, inc="PE" if j == 7 else None)
                        dst3 = wst[:, b * 1024:(b + 1) * 1024].rearrange(
                            "p (k d) -> p k d", d=128)
                        src3 = p_tr.rearrange("p (j d) -> p j d", d=128)
                        ev = P.emit("vector", _cp(dst3, src3), wait=[("PE", tr)],
                                    inc="VE")
                        tr_done("VE", ev)
                    if kind == "d":
                        dd = dst4[q * 16:(q + 1) * 16, rb, :, :].rearrange(
                            "ic p j -> p ic j")
                    else:
                        dd = dst4[rb, :, :, :].rearrange("kc p j -> p kc j")
                    src3 = wst.rearrange("p (k d) -> p k d", d=128)
                    st = P.emit("gpsimd", _dma(dd, src3), wait=[("VE", ev)],
                                inc="wS", inc_by=16)
                    p3_state["wst_free"][sthf] = [("wS", st)]

        ks = {"ktr_pe": {}, "ld_free": [list(p2_done), list(p2_done)],
              "ksplit": {}, "ktr_ve": {}, "vsplit": {}, "qld": {},
              "kfm_rd": {}, "pv_done": {0: 0, 1: 0}, "scores_done": {},
              "q_slot_free": {}, "sc_grp": {}}
        score_free = {0: list(p2_done), 1: list(p2_done)}
        pv_free = {0: [], 1: []}
        pt_free = [[], []]
        ahst_free = [[], []]
        p32_half_free = {0: [], 1: []}
        prev_exp_last = {}
        pend_pv = []

        def emit_k_group(h, g):
            gf = g % 2
            k3 = kvk_d.rearrange("(kc p) d -> p kc d", p=128)[
                :, g * 8:(g + 1) * 8, h * 128:(h + 1) * 128]
            dstk = kvt32[:, gf * 1024:(gf + 1) * 1024]
            dh = kvt_h[:, gf * 1024:(gf + 1) * 1024]
            dl = kvt_l[:, gf * 1024:(gf + 1) * 1024]
            ksem = "kL%d" % gf
            ld = P.emit("sync", _dma(
                dstk.rearrange("p (kc d) -> p kc d", d=128), k3),
                wait=ks["ld_free"][gf], inc=ksem, inc_by=16)
            wk0 = P.fence("SE", "fS", "qS") if h == 0 else []
            if (h, g - 2) in ks["ktr_pe"]:
                wk0 = wk0 + [("PE", ks["ktr_pe"][(h, g - 2)])]
            sh_ = P.emit("scalar", _scp(dh, dstk), wait=[(ksem, ld)] + wk0,
                         inc="SE")
            gl = P.emit("gpsimd", _sub(dl, dstk, dh), wait=[("SE", sh_)],
                        inc="GE")
            ks["ld_free"][gf] = [("GE", gl)]
            ks["ksplit"][(h, g)] = (sh_, gl)

        def emit_v_prep(h):
            wv0 = [("PE", ks["pv_done_all"])] if h > 0 else \
                P.fence("SE", "fS", "qS")
            for g in range(4):
                gf = g % 2
                v3 = kvv_d.rearrange("(kc p) d -> p kc d", p=128)[
                    :, g * 8:(g + 1) * 8, h * 128:(h + 1) * 128]
                dstk = kvt32[:, gf * 1024:(gf + 1) * 1024]
                ksem = "kL%d" % gf
                ld = P.emit("sync", _dma(
                    dstk.rearrange("p (kc d) -> p kc d", d=128), v3),
                    wait=ks["ld_free"][gf], inc=ksem, inc_by=16)
                sh_ = P.emit("scalar", _scp(vtm_h[:, g * 1024:(g + 1) * 1024],
                                            dstk),
                             wait=[(ksem, ld)] + (wv0 if g == 0 else []),
                             inc="SE")
                gl = P.emit("gpsimd", _sub(vtm_l[:, g * 1024:(g + 1) * 1024],
                                           dstk,
                                           vtm_h[:, g * 1024:(g + 1) * 1024]),
                            wait=[("SE", sh_)], inc="GE")
                ks["ld_free"][gf] = [("GE", gl)]
            ks["vsplit"][h] = ("SE", P.now("SE"), P.now("GE"))

        def emit_q_load(g):
            h, qb = g // 8, g % 8
            slot = g % 4
            qsem = "qL%d" % slot
            wv = list(ks["q_slot_free"].get(slot, []))
            if g == 0:
                wv += [("qS", P.now("qS")), ("PE", P.now("PE"))]
            P.emit("sync", _dma(
                qh_b[:, slot * 256:(slot + 1) * 256],
                q_h_d[h * 128:(h + 1) * 128, qb * 256:(qb + 1) * 256]),
                wait=wv, inc=qsem, inc_by=16)
            q2 = P.emit("sync", _dma(
                ql_b[:, slot * 256:(slot + 1) * 256],
                q_l_d[h * 128:(h + 1) * 128, qb * 256:(qb + 1) * 256]),
                inc=qsem, inc_by=16)
            ks["qld"][g] = (qsem, q2)

        def emit_ktr_group_one(h, g):
            hp = h % 2
            gf = g % 2
            sh_, gl = ks["ksplit"][(h, g)]
            wv0 = [("PE", ks["kfm_rd"][h - 2])] if h >= 2 and ks["kfm_rd"][h - 2] else []
            for part, (sb, dst) in enumerate(((kvt_h, kfm_h), (kvt_l, kfm_l))):
                tr = None
                for j in range(8):
                    wv = []
                    if j == 0:
                        wv = [("SE", sh_), ("GE", gl)] + tr_wait() + wv0
                    tr = P.emit("tensor", _tr(
                        p_tr[:, j * 128:(j + 1) * 128],
                        sb[:, gf * 1024 + j * 128:gf * 1024 + (j + 1) * 128],
                        sid_bf[:, :], j == 0, j == 7),
                        wait=wv, inc="PE" if j == 7 else None)
                dst2 = dst[:, hp * KVN + g * 1024:hp * KVN + (g + 1) * 1024]
                dst3 = dst2.rearrange("p (j d) -> p j d", d=128)
                src3 = p_tr.rearrange("p (j d) -> p j d", d=128)
                ev = P.emit("scalar", _scp(dst3, src3), wait=[("PE", tr)],
                            inc="SE")
                tr_done("SE", ev)
            ks["ktr_ve"][h] = ("SE", P.now("SE"))

        emit_k_group(0, 0)
        emit_k_group(0, 1)
        emit_ktr_group_one(0, 0)
        emit_k_group(0, 2)
        emit_ktr_group_one(0, 1)
        emit_k_group(0, 3)
        emit_ktr_group_one(0, 2)
        emit_ktr_group_one(0, 3)
        emit_v_prep(0)
        for g0 in range(3):
            emit_q_load(g0)

        for h in range(HEADS):
            hp = h % 2
            kread = ks["ktr_ve"][h]
            vspl = ks["vsplit"][h]
            pv_group_pe = {}
            for qb in range(8):
                qbp = qb % 2
                gq = h * 8 + qb
                qld = ks["qld"][gq]
                qrs = (gq % 4) * 256
                sc_pe_start = P.now("PE")
                spl = {}
                for qcx in range(2):
                    if qcx == 1 and pend_pv:
                        emit_pv_part(pend_pv[0], 0, 8)
                    qc = qb * 2 + qcx
                    qcp = qc % 2
                    qslot = qbp * 2 + qcx
                    qsl = qh_b[:, qrs + qcx * 128:qrs + (qcx + 1) * 128]
                    qsll = ql_b[:, qrs + qcx * 128:qrs + (qcx + 1) * 128]
                    sm = 9 * qcp
                    exps = []
                    vm2 = None
                    for kb in range(4):
                        kbp = kb % 2
                        reg = p_f[:, (2 * kbp) * 512:(2 * kbp + 2) * 512]
                        mm = None
                        for kt in range(2):
                            dstp = reg[:, kt * 512:(kt + 1) * 512]
                            for pi, (qop, kop) in enumerate(
                                    ((qsl, kfm_h), (qsll, kfm_h), (qsl, kfm_l))):
                                wv = []
                                if kt == 0 and pi == 0:
                                    wv = score_free[kbp] + [kread, qld]
                                rhs = kop[:, hp * KVN + kb * 1024 + kt * 512:
                                          hp * KVN + kb * 1024 + (kt + 1) * 512]
                                mm = P.emit("tensor",
                                            _mm(dstp, qop, rhs, pi == 0, pi == 2),
                                            wait=wv,
                                            inc="PE" if (kt == 1 and pi == 2)
                                            else None)
                        if kb == 0:
                            wvm = [("PE", mm)]
                            if prev_exp_last.get(qcp):
                                wvm.append(("SE", prev_exp_last[qcp]))
                            P.emit("vector", _rmax(smax[:, sm:sm + 1], reg, True),
                                   wait=wvm, inc="VE")
                            P.emit("vector", _cp(smax[:, 20:21], sconst[:, 6:7]),
                                   inc="VE")
                            vm2 = P.emit("vector",
                                         _tsm(smax[:, sm + 1:sm + 2],
                                              smax[:, sm:sm + 1], sconst[:, 5:6]),
                                         inc="VE")
                        ex = P.emit("scalar", _act(
                            p32[:, kb * 1024:(kb + 1) * 1024], reg, AF.Exp,
                            bias=smax[:, sm + 1:sm + 2], scale=sconst[:, 5:6],
                            acc=smax[:, sm + 2 + kb:sm + 3 + kb]),
                            wait=[("PE", mm), ("VE", vm2)] +
                            p32_half_free[kb // 2],
                            inc="SE")
                        exps.append(ex)
                        score_free[kbp] = [("SE", ex)]
                    prev_exp_last[qcp] = exps[-1]
                    P.emit("vector", _add(smax[:, sm + 6:sm + 7],
                                          smax[:, sm + 2:sm + 3],
                                          smax[:, sm + 3:sm + 4]),
                           wait=[("SE", exps[-1])], inc="VE")
                    P.emit("vector", _add(smax[:, sm + 7:sm + 8],
                                          smax[:, sm + 4:sm + 5],
                                          smax[:, sm + 5:sm + 6]), inc="VE")
                    P.emit("vector", _cp(smax[:, 20:21], sconst[:, 6:7]),
                           inc="VE")
                    P.emit("vector", _add(smax[:, sm + 2:sm + 3],
                                          smax[:, sm + 6:sm + 7],
                                          smax[:, sm + 7:sm + 8]), inc="VE")
                    P.emit("vector", _cp(smax[:, 20:21], sconst[:, 6:7]),
                           inc="VE")
                    vr = P.emit("vector", _recip(smax[:, sm + 8:sm + 9],
                                                 smax[:, sm + 2:sm + 3]),
                                inc="VE")
                    P.emit("vector", _cp(smax[:, 20:21], sconst[:, 6:7]),
                           inc="VE")
                    phs = ph[:, qslot * KVN:(qslot + 1) * KVN]
                    pls = pl[:, qslot * KVN:(qslot + 1) * KVN]
                    wv = [("VE", vr)]
                    if ks["pv_done"][qbp]:
                        wv.append(("PE", ks["pv_done"][qbp]))
                    vpl = None
                    for qt in range(2):
                        qs_ = slice(qt * 2048, (qt + 1) * 2048)
                        P.emit("vector", _tsm(phs[:, qs_], p32[:, qs_],
                                              smax[:, sm + 8:sm + 9]),
                               wait=(wv if qt == 0 else []), inc="VE")
                        vpl = P.emit("vector", _stt(pls[:, qs_], p32[:, qs_],
                                                    smax[:, sm + 8:sm + 9],
                                                    phs[:, qs_],
                                                    ALU.mult, ALU.subtract),
                                     inc="VE")
                        p32_half_free[qt] = [("VE", vpl)]
                    spl[qcx] = vpl

                def emit_pv_part(st, lo, hi):
                    h, qb, spl0, spl1, vspl = st["args"]
                    qbp = qb % 2
                    pvdst = fb(4 + qbp, 0, 256)
                    pend_mm = st["pend_mm"]

                    def pv_tr(kg):
                        kgp = kg % 2
                        kgs = kg % 4
                        ptr = p_tr if kgp == 0 else p_tr_b
                        ptrw = tr_wait if kgp == 0 else trb_wait
                        ptrd = tr_done if kgp == 0 else trb_done
                        tr = None
                        for kcg in range(2):
                            kc = kg * 2 + kcg
                            for part, psrc in enumerate((ph, pl)):
                                for qcx in range(2):
                                    qslot = qbp * 2 + qcx
                                    j = kcg * 4 + part * 2 + qcx
                                    wv = []
                                    if j == 0 and kcg == 0:
                                        wv = [("VE", spl0), ("VE", spl1)] + \
                                            ptrw() + pt_free[kgp]
                                    tr = P.emit("tensor", _tr(
                                        ptr[:, j * 128:(j + 1) * 128],
                                        psrc[:, qslot * KVN + kc * 128:
                                             qslot * KVN + (kc + 1) * 128],
                                        sid_bf[:, :], j == 0, j == 7),
                                        wait=wv, inc="PE" if j == 7 else None)
                        ptd = pt[:, kgs * 1024:(kgs + 1) * 1024]
                        pev = P.emit("scalar", _scp(ptd, ptr[:, :]),
                                     wait=[("PE", tr)], inc="SE")
                        psem = "SE"
                        ptrd(psem, pev)
                        return kg, kgp, ptd, psem, pev

                    def pv_mm(kg, kgp, ptd, psem, pev):
                        for kcg in range(2):
                            kc = kg * 2 + kcg
                            vh_t = vtm_h[:, kc * 128:(kc + 1) * 128]
                            vl_t = vtm_l[:, kc * 128:(kc + 1) * 128]
                            rh = ptd[:, kcg * 512:kcg * 512 + 256]
                            rl = ptd[:, kcg * 512 + 256:kcg * 512 + 512]
                            for pi, (vop, rop) in enumerate(
                                    ((vh_t, rh), (vl_t, rh), (vh_t, rl))):
                                wv = [(psem, pev)]
                                if kg == 0 and kcg == 0 and pi == 0:
                                    wv += [("SE", vspl[1]), ("GE", vspl[2])] + \
                                        pv_free[qbp]
                                mm = P.emit("tensor", _mm(
                                    pvdst, vop, rop,
                                    kg == 0 and kcg == 0 and pi == 0,
                                    kg == 15 and kcg == 1 and pi == 2),
                                    wait=wv,
                                    inc="PE" if (kcg == 1 and pi == 2) else None)
                        pt_free[kgp] = [("PE", mm)]
                        st["mm"] = mm

                    for kg in range(lo, hi):
                        trs = pv_tr(kg)
                        if len(pend_mm) >= 3:
                            pv_mm(*pend_mm.pop(0))
                        pend_mm.append(trs)
                    if hi == 16:
                        while pend_mm:
                            pv_mm(*pend_mm.pop(0))
                        emit_pv_finish(st)

                def emit_pv_finish(st):
                    h, qb, spl0, spl1, vspl = st["args"]
                    qbp = qb % 2
                    mm = st["mm"]
                    ks["pv_done"][qbp] = mm
                    ap_h = ahst_h[:, qbp * 256:(qbp + 1) * 256]
                    ap_l = ahst_l[:, qbp * 256:(qbp + 1) * 256]
                    pvrd = fb(4 + qbp, 0, 256)
                    se1 = P.emit("scalar", _scp(ap_h, pvrd),
                                 wait=[("PE", mm)] + ahst_free[qbp], inc="SE")
                    ve1 = P.emit("vector", _stt(ap_l, pvrd, sconst[:, 6:7],
                                                ap_h, ALU.mult, ALU.subtract),
                                 wait=[("SE", se1)], inc="VE")
                    pv_free[qbp] = [("SE", se1), ("VE", ve1)]
                    g1 = P.emit("gpsimd", _dma(
                        ah_h_d[h * 128:(h + 1) * 128, qb * 256:(qb + 1) * 256],
                        ap_h), wait=[("SE", se1)], inc="aS", inc_by=16)
                    g2 = P.emit("gpsimd", _dma(
                        ah_l_d[h * 128:(h + 1) * 128, qb * 256:(qb + 1) * 256],
                        ap_l), wait=[("VE", ve1)], inc="aS", inc_by=16)
                    ahst_free[qbp] = [("aS", g2)]

                if pend_pv:
                    emit_pv_part(pend_pv[0], 8, 16)
                    pend_pv.pop(0)
                ks["sc_grp"][gq] = P.now("PE")
                ks["q_slot_free"][gq % 4] = [("PE", ks["sc_grp"][gq])]
                if gq + 3 < HEADS * 8:
                    emit_q_load(gq + 3)
                if DEBUG and h == 0 and qb == 0:
                    P.emit("gpsimd", _dma(kfm_dbg[:, 0:KVN], kfm_h[:, 0:KVN]),
                           wait=[kread], inc="dS", inc_by=16)
                    P.emit("gpsimd", _dma(kfm_dbg[:, KVN:2 * KVN],
                                          kfm_l[:, 0:KVN]), inc="dS", inc_by=16)
                    P.emit("gpsimd", _dma(vtm_dbg[:, 0:KVN], vtm_h[:, :]),
                           wait=[("SE", vspl[1]), ("GE", vspl[2])],
                           inc="dS", inc_by=16)
                    P.emit("gpsimd", _dma(vtm_dbg[:, KVN:2 * KVN], vtm_l[:, :]),
                           inc="dS", inc_by=16)
                    P.emit("gpsimd", _dma(p_dbg[:, 0:KVN], ph[:, 0:KVN]),
                           wait=[("VE", spl[1])], inc="dS", inc_by=16)
                    P.emit("gpsimd", _dma(p_dbg[:, KVN:2 * KVN], pl[:, 0:KVN]),
                           inc="dS", inc_by=16)
                    P.emit("gpsimd", _dma(p_dbg[:, 2 * KVN:3 * KVN],
                                          ph[:, KVN:2 * KVN]), inc="dS", inc_by=16)
                    P.emit("gpsimd", _dma(p_dbg[:, 3 * KVN:4 * KVN],
                                          pl[:, KVN:2 * KVN]), inc="dS", inc_by=16)
                pend_pv.append({"args": (h, qb, spl[0], spl[1], vspl),
                                "pend_mm": [], "mm": None})
                if h + 1 < HEADS and qb < 4:
                    emit_k_group(h + 1, qb)
                    if qb >= 1:
                        emit_ktr_group_one(h + 1, qb - 1)
                if h + 1 < HEADS and qb == 4:
                    emit_ktr_group_one(h + 1, 3)
                if 1 <= qb <= 6:
                    emit_p3(2)
            ks["scores_done"][h] = P.now("PE")
            while pend_pv:
                emit_pv_part(pend_pv[0], 0, 16)
                pend_pv.pop(0)
            ks["pv_done_all"] = P.now("PE")
            ks["kfm_rd"][h] = P.now("PE")
            if h + 1 < HEADS:
                emit_v_prep(h + 1)
        emit_p3(len(p3_units))
        p4_done = P.fence()

        if STAGE < 5:
            P.emit("gpsimd", _memset(smax[:, 15:16], 0.0),
                   wait=P.fence("xS", "qS", "fS", "wS", "aS", "o5S", "o6S",
                                "dS"),
                   inc="GE")
            _finish(nc, ctx, P)
            return nc
        wo_free = [list(p4_done), list(p4_done)]
        ab_free = [[], []]
        ostg5_free = [[], []]
        evst5_free = []
        bank5_free = {0: [], 1: []}
        woTb_rd = {}
        for oc in range(NFC):
            hf = oc % 2
            dh = woTb_h[:, hf * D:(hf + 1) * D]
            dl = woTb_l[:, hf * D:(hf + 1) * D]
            ws = wrow5[:, hf * D:(hf + 1) * D]
            sh_b = wsp5_h[:, hf * D:(hf + 1) * D]
            sl_b = wsp5_l[:, hf * D:(hf + 1) * D]
            osem = "oL%d" % hf
            ld = P.emit("sync", _dma(ws, wo_d[oc * 128:(oc + 1) * 128, :]),
                        wait=wo_free[hf], inc=osem, inc_by=16)
            gh = P.emit("gpsimd", _cp(sh_b, ws),
                        wait=[(osem, ld)] + (p4_done if oc < 2 else []), inc="GE")
            gl = P.emit("gpsimd", _sub(sl_b, ws, sh_b), inc="GE")
            wo_free[hf] = [("GE", gl)]
            ev = None
            for part, (sb, dst) in enumerate(((sh_b, dh), (sl_b, dl))):
                for b in range(2):
                    tr = None
                    for j in range(8):
                        k = b * 8 + j
                        wv = []
                        if j == 0:
                            wv = [("GE", gl)] + tr_wait()
                            if oc >= 2 and b == 0 and part == 0:
                                wv.append(("PE", woTb_rd[oc - 2]))
                        tr = P.emit("tensor",
                                    _tr(p_tr[:, j * 128:(j + 1) * 128],
                                        sb[:, k * 128:(k + 1) * 128],
                                        sid_bf[:, :], j == 0, j == 7),
                                    wait=wv, inc="PE" if j == 7 else None)
                    dst3 = dst[:, b * 1024:(b + 1) * 1024].rearrange(
                        "p (j d) -> p j d", d=128)
                    src3 = p_tr.rearrange("p (j d) -> p j d", d=128)
                    ev = P.emit("vector", _cp(dst3, src3), wait=[("PE", tr)],
                                inc="VE")
                    tr_done("VE", ev)
            for tt in range(4):
                ttf = tt % 2
                lds = []
                for half, (dd, dstb) in enumerate(((ah_h_d, absb_h),
                                                   (ah_l_d, absb_l))):
                    src3 = dd.rearrange("(kc p) t -> p kc t", p=128)[
                        :, :, tt * 512:(tt + 1) * 512]
                    dst2 = dstb[:, ttf * NFC * 512:(ttf + 1) * NFC * 512]
                    dst3 = dst2.rearrange("p (kc t) -> p kc t", t=512)
                    wv = list(ab_free[ttf]) + (p4_done if oc == 0 and tt < 2 else [])
                    lds.append(P.emit("sync", _dma(dst3, src3), wait=wv,
                                      inc="aL%d" % ttf, inc_by=16))
                bank = (oc * 4 + tt) % 2
                dstp = fb(bank)
                abh = absb_h[:, ttf * NFC * 512:(ttf + 1) * NFC * 512]
                abl = absb_l[:, ttf * NFC * 512:(ttf + 1) * NFC * 512]
                mm = None
                for kc in range(NFC):
                    for pi, (wop, aop) in enumerate(((dh, abh), (dl, abh),
                                                     (dh, abl))):
                        wv = []
                        if kc == 0 and pi == 0:
                            wv = [("VE", ev), ("aL%d" % ttf, lds[1])] + bank5_free[bank]
                        mm = P.emit("tensor", _mm(
                            dstp, wop[:, kc * 128:(kc + 1) * 128],
                            aop[:, kc * 512:(kc + 1) * 512],
                            kc == 0 and pi == 0, kc == NFC - 1 and pi == 2),
                            wait=wv,
                            inc="PE" if kc == NFC - 1 and pi == 2 else None)
                ab_free[ttf] = [("PE", mm)]
                se1 = P.emit("scalar", _scp(evst5[:, :], dstp),
                             wait=[("PE", mm)] + evst5_free, inc="SE")
                bank5_free[bank] = [("SE", se1)]
                trl = None
                for j in range(4):
                    wv = []
                    if j == 0:
                        wv = [("SE", se1)] + t6_wait()
                    trl = P.emit("tensor", _tr(fb(6, j * 128, 128),
                                               evst5[:, j * 128:(j + 1) * 128],
                                               sid_f32[:, :], j == 0, j == 3),
                                 wait=wv, inc="PE" if j == 3 else None)
                ost = ostg5[:, ttf * 512:(ttf + 1) * 512]
                ev2 = P.emit("vector", _cp(ost, fb(6)),
                             wait=[("PE", trl)] + ostg5_free[ttf], inc="VE")
                t6_done("VE", ev2)
                evst5_free = [("VE", ev2)]
                dst3 = out_d.rearrange("(tc p) d -> p tc d", p=128)[
                    :, tt * 4:(tt + 1) * 4, oc * 128:(oc + 1) * 128]
                src3 = ost.rearrange("p (tc d) -> p tc d", d=128)
                g1 = P.emit("gpsimd", _dma_acc(dst3, src3), wait=[("VE", ev2)],
                            inc="o5S", inc_by=16)
                ostg5_free[ttf] = [("o5S", g1)]
            woTb_rd[oc] = P.now("PE")
        p5_done = P.fence()

        if STAGE < 6:
            P.emit("gpsimd", _memset(smax[:, 15:16], 0.0),
                   wait=P.fence("xS", "qS", "fS", "wS", "aS", "o5S", "o6S",
                                "dS"),
                   inc="GE")
            _finish(nc, ctx, P)
            return nc
        down_bank_evac = {}
        h1_free = []
        bu_free = {0: list(p5_done), 1: []}
        bg_free = {0: [], 1: []}
        bz_free = {0: [], 1: []}
        wug_u_free = []
        wug_g_free = []
        wdb_free = [list(p5_done), list(p5_done)]
        z2_free = [[], []]
        evst6_free = []
        ostg6_free = [[], []]
        hsu6_pe = {}
        pend_z = []
        bf_lvl = {}
        bf_bv = {"v": 0}
        OCG = ((0, 6), (6, 12), (12, 16))

        def bf_cascade(newtile):
            bf_lvl[newtile] = 0
            first_wait = [("SE", z1_ev[newtile])]
            while True:
                fired = False
                for l in range(6):
                    hstep = 1 << l
                    for c in range(NIC):
                        if c & hstep:
                            continue
                        if bf_lvl.get(c) == l and bf_lvl.get(c + hstep) == l:
                            a = z1[:, c * 512:(c + 1) * 512]
                            b = z1[:, (c + hstep) * 512:(c + hstep + 1) * 512]
                            P.emit("vector", _sub(b, a, b), wait=first_wait,
                                   inc=None)
                            bf_bv["v"] = P.emit("vector",
                                                _stt(a, a, sconst[:, 2:3], b,
                                                     ALU.mult, ALU.subtract),
                                                inc="VE")
                            first_wait = []
                            bf_lvl[c] = l + 1
                            bf_lvl[c + hstep] = l + 1
                            fired = True
                if not fired:
                    break
        for tt in range(4):
            lh = []
            for half, (dd, dstb) in enumerate(((h1_h_d, h1h_s), (h1_l_d, h1l_s))):
                src3 = dd.rearrange("(kc p) t -> p kc t", p=128)[
                    :, :, tt * 512:(tt + 1) * 512]
                dst3 = dstb.rearrange("p (kc t) -> p kc t", t=512)
                lh.append(P.emit("sync", _dma(dst3, src3),
                                 wait=(p5_done if tt == 0 else []) + h1_free,
                                 inc="hL", inc_by=16))
            h1lod = lh[-1]
            z1_ev = {}
            bf_lvl.clear()
            bf_bv["v"] = 0
            for ic in range(NIC):
                icp = ic % 2
                ldu1 = P.emit("sync", _dma(
                    wug[:, 0:2048].rearrange("p (kc j) -> p kc j", j=128),
                    wupT_h_d[ic, :, :, :].rearrange("kc p j -> p kc j")),
                    wait=wug_u_free + (p5_done if tt == 0 and ic == 0 else []),
                    inc="uL", inc_by=16)
                ldu = P.emit("sync", _dma(
                    wug[:, 2048:4096].rearrange("p (kc j) -> p kc j", j=128),
                    wupT_l_d[ic, :, :, :].rearrange("kc p j -> p kc j")),
                    inc="uL", inc_by=16)
                bu = fb(0 if icp == 0 else 2)
                mmu = None
                for kc in range(NFC):
                    for pi in range(3):
                        wof = 0 if pi != 1 else 2048
                        xop = h1h_s if pi != 2 else h1l_s
                        wop = wug[:, wof + kc * 128:wof + (kc + 1) * 128]
                        wv = []
                        if kc == 0 and pi == 0:
                            wv = [("uL", ldu), ("hL", h1lod)] + bu_free[icp]
                        mmu = P.emit("tensor", _mm(
                            bu, wop, xop[:, kc * 512:(kc + 1) * 512],
                            kc == 0 and pi == 0, kc == NFC - 1 and pi == 2),
                            wait=wv,
                            inc="PE" if kc == NFC - 1 and pi == 2 else None)
                wug_u_free = [("PE", mmu)]
                ldg1 = P.emit("sync", _dma(
                    wug[:, 4096:6144].rearrange("p (kc j) -> p kc j", j=128),
                    wgT_h_d[ic, :, :, :].rearrange("kc p j -> p kc j")),
                    wait=wug_g_free + (p5_done if tt == 0 and ic == 0 else []),
                    inc="gL", inc_by=16)
                ldg = P.emit("sync", _dma(
                    wug[:, 6144:8192].rearrange("p (kc j) -> p kc j", j=128),
                    wgT_l_d[ic, :, :, :].rearrange("kc p j -> p kc j")),
                    inc="gL", inc_by=16)
                bg = fb(1 if icp == 0 else 3)
                mmg = None
                for kc in range(NFC):
                    for pi in range(3):
                        wof = 4096 if pi != 1 else 6144
                        xop = h1h_s if pi != 2 else h1l_s
                        wop = wug[:, wof + kc * 128:wof + (kc + 1) * 128]
                        wv = []
                        if kc == 0 and pi == 0:
                            wv = [("gL", ldg)] + bg_free[icp]
                        mmg = P.emit("tensor", _mm(
                            bg, wop, xop[:, kc * 512:(kc + 1) * 512],
                            kc == 0 and pi == 0, kc == NFC - 1 and pi == 2),
                            wait=wv,
                            inc="PE" if kc == NFC - 1 and pi == 2 else None)
                wug_g_free = [("PE", mmg)]
                se1 = P.emit("scalar", _act(silu_b[:, :], bg, AF.Silu,
                                            scale=svec[:, 80 + ic:81 + ic]),
                             wait=[("PE", mmg)], inc="SE")
                bg_free[icp] = [("SE", se1)]
                va = P.emit("vector", _stt(a32[:, :], bu,
                                           svec[:, 16 + ic:17 + ic],
                                           silu_b[:, :], ALU.mult, ALU.mult),
                            wait=[("SE", se1), ("PE", mmu)], inc="VE")
                bu_free[icp] = [("VE", va)]
                aht = a_h_t[:, icp * 512:(icp + 1) * 512]
                alt = a_l_t[:, icp * 512:(icp + 1) * 512]
                wvs = [("VE", va)]
                if hsu6_pe.get((tt, ic - 1)):
                    wvs.append(("PE", hsu6_pe[(tt, ic - 1)]))
                sh1 = P.emit("scalar", _scp(aht, a32[:, :]),
                             wait=wvs, inc="SE")
                vl1 = P.emit("vector", _stt(alt, a32[:, :],
                                            sconst[:, 6:7], aht,
                                            ALU.mult, ALU.subtract),
                             wait=[("SE", sh1)], inc="VE")
                hr = hsu_r[:, icp * 128:(icp + 1) * 128]
                wvh = [("cL", ld0)]
                if tt == 0 and ic < 2:
                    wvh += p5_done
                if hsu6_pe.get((tt, ic - 2)) or (tt > 0 and ic < 2):
                    wvh.append(("PE", hsu6_pe.get((tt, ic - 2),
                                                  P.now("PE"))))
                hv = P.emit("vector", _tsm(hr, sh128[:, :],
                                           svec[:, 160 + ic:161 + ic]),
                            wait=wvh, inc="VE")

                def emit_zstage(ic, icp, hr, aht, alt, hv, sh1, vl1):
                    bz = fb(4 if icp == 0 else 5)
                    P.emit("tensor", _mm(bz, hr, aht, True, False),
                           wait=[("VE", hv), ("SE", sh1)] + bz_free[icp],
                           inc=None)
                    m2 = P.emit("tensor", _mm(bz, hr, alt, False, True),
                                wait=[("VE", vl1)], inc="PE")
                    hsu6_pe[(tt, ic)] = m2
                    ze = P.emit("scalar",
                                _scp(z1[:, ic * 512:(ic + 1) * 512], bz),
                                wait=[("PE", m2)], inc="SE")
                    bz_free[icp] = [("SE", ze)]
                    z1_ev[ic] = ze

                pend_z.append((ic, icp, hr, aht, alt, hv, sh1, vl1))
                if len(pend_z) > 1:
                    emit_zstage(*pend_z.pop(0))
                    bf_cascade(ic - 1)
            while pend_z:
                st_ = pend_z.pop(0)
                emit_zstage(*st_)
                bf_cascade(st_[0])
            assert all(bf_lvl[c] == 6 for c in range(NIC)), bf_lvl
            bv = bf_bv["v"]
            for gi, (o0, o1) in enumerate(OCG):
                nb = o1 - o0
                mm = None
                for ic in range(NIC):
                    icp = ic % 2
                    wdsl = wdb[:, icp * 1536:icp * 1536 + nb * 128]
                    wdsl_l = wdb[:, icp * 1536 + 768:icp * 1536 + 768 + nb * 128]
                    dsem = "dL%d" % icp
                    P.emit("sync", _dma(
                        wdsl.rearrange("p (o j) -> p o j", j=128),
                        wdT_h_d[ic, o0:o1, :, :].rearrange("o p j -> p o j")),
                        wait=wdb_free[icp], inc=dsem, inc_by=16)
                    ldd = P.emit("sync", _dma(
                        wdsl_l.rearrange("p (o j) -> p o j", j=128),
                        wdT_l_d[ic, o0:o1, :, :].rearrange("o p j -> p o j")),
                        inc=dsem, inc_by=16)
                    zh = z2h[:, icp * 512:(icp + 1) * 512]
                    zl = z2l[:, icp * 512:(icp + 1) * 512]
                    s1 = P.emit("scalar", _scp(zh, z1[:, ic * 512:(ic + 1) * 512]),
                                wait=[("VE", bv)] + z2_free[icp], inc="SE")
                    v1 = P.emit("vector", _stt(zl, z1[:, ic * 512:(ic + 1) * 512],
                                               sconst[:, 6:7], zh,
                                               ALU.mult, ALU.subtract),
                                wait=[("SE", s1)], inc="VE")
                    for oi in range(nb):
                        dstp = fb(oi)
                        for pi in range(3):
                            wop = (wdsl if pi != 1 else wdsl_l)[
                                :, oi * 128:(oi + 1) * 128]
                            rop = zh if pi != 2 else zl
                            wv = [(dsem, ldd), ("SE", s1)]
                            if pi == 2:
                                wv.append(("VE", v1))
                            if ic == 0 and pi == 0:
                                se_prev = down_bank_evac.get(oi)
                                if se_prev:
                                    wv.append(("SE", se_prev))
                            mm = P.emit("tensor", _mm(
                                dstp, wop, rop, ic == 0 and pi == 0,
                                ic == NIC - 1 and pi == 2),
                                wait=wv,
                                inc="PE" if (oi == nb - 1 and pi == 2)
                                else None)
                    z2_free[icp] = [("PE", mm)]
                    wdb_free[icp] = [("PE", mm)]
                dn_pe = P.now("PE")
                for oi in range(nb):
                    oc = o0 + oi
                    se2 = P.emit("scalar", _act(evst6[:, :], fb(oi), AF.Copy,
                                                scale=svec[:, 144 + oc:145 + oc]),
                                 wait=[("PE", dn_pe)] + evst6_free, inc="SE")
                    down_bank_evac[oi] = se2
                    trl = None
                    for j in range(4):
                        wv = []
                        if j == 0:
                            wv = [("SE", se2)] + t6_wait()
                        trl = P.emit("tensor", _tr(fb(6, j * 128, 128),
                                                   evst6[:, j * 128:(j + 1) * 128],
                                                   sid_f32[:, :], j == 0, j == 3),
                                     wait=wv, inc="PE" if j == 3 else None)
                    ost = ostg6[:, (oi % 2) * 512:(oi % 2 + 1) * 512]
                    ev2 = P.emit("vector", _cp(ost, fb(6)),
                                 wait=[("PE", trl)] + ostg6_free[oi % 2],
                                 inc="VE")
                    t6_done("VE", ev2)
                    evst6_free = [("VE", ev2)]
                    dst3 = out_d.rearrange("(tc p) d -> p tc d", p=128)[
                        :, tt * 4:(tt + 1) * 4, oc * 128:(oc + 1) * 128]
                    src3 = ost.rearrange("p (tc d) -> p tc d", d=128)
                    g1 = P.emit("gpsimd", _dma_acc(dst3, src3),
                                wait=[("VE", ev2)], inc="o6S", inc_by=16)
                    ostg6_free[oi % 2] = [("o6S", g1)]
            h1_free = [("PE", P.now("PE"))]
            end_f = [("SE", P.now("SE")), ("VE", P.now("VE"))]
            bu_free = {0: list(end_f), 1: list(end_f)}
            bg_free = {0: list(end_f), 1: list(end_f)}
            bz_free = {0: list(end_f), 1: list(end_f)}
        P.emit("gpsimd", _memset(smax[:, 15:16], 0.0),
               wait=P.fence("xS", "qS", "fS", "wS", "aS", "o5S", "o6S", "dS"),
               inc="GE")

        _finish(nc, ctx, P)
    return nc


_NC_CACHE = {}


def kernel(L0_attn_query, L0_attn_kv_k, L0_attn_kv_v, L0_attn_Wq, L0_attn_Wo,
           L0_ffn_SU_ug, L0_ffn_SU_dp, L0_ffn_Wscale_ug, L0_ffn_Wscale_dp,
           L0_ffn_W_up, L0_ffn_W_gate, L0_ffn_W_down):
    x = np.ascontiguousarray(np.asarray(L0_attn_query, dtype=np.float32)
                             .reshape(16384, D))
    shared = {
        "kvk": np.ascontiguousarray(np.asarray(L0_attn_kv_k, dtype=np.float32)
                                    .reshape(KVN, D)),
        "kvv": np.ascontiguousarray(np.asarray(L0_attn_kv_v, dtype=np.float32)
                                    .reshape(KVN, D)),
        "wq": np.ascontiguousarray(np.asarray(L0_attn_Wq, dtype=np.float32)),
        "wo": np.ascontiguousarray(np.asarray(L0_attn_Wo, dtype=np.float32)),
        "su_ug": np.ascontiguousarray(np.asarray(L0_ffn_SU_ug, dtype=np.float32)),
        "su_dp": np.ascontiguousarray(np.asarray(L0_ffn_SU_dp, dtype=np.float32)),
        "wscale_ug": np.ascontiguousarray(np.asarray(L0_ffn_Wscale_ug,
                                                     dtype=np.float32)),
        "wscale_dp": np.ascontiguousarray(np.asarray(L0_ffn_Wscale_dp,
                                                     dtype=np.float32)),
        "wup": np.ascontiguousarray(np.asarray(L0_ffn_W_up, dtype=np.float32)),
        "wg": np.ascontiguousarray(np.asarray(L0_ffn_W_gate, dtype=np.float32)),
        "wd": np.ascontiguousarray(np.asarray(L0_ffn_W_down, dtype=np.float32)),
    }
    if "nc" not in _NC_CACHE:
        _NC_CACHE["nc"] = build_nc()
    nc = _NC_CACHE["nc"]
    in_maps = []
    for c in range(8):
        m = dict(shared)
        m["x"] = np.ascontiguousarray(x[c * T:(c + 1) * T, :])
        in_maps.append(m)
    res = run_bass_kernel_spmd(nc, in_maps, core_ids=list(range(8)))
    out = np.concatenate([np.asarray(r["out"]) for r in res.results], axis=0)
    return out.reshape(1, 16384, D).astype(np.float32)
